# version 07
# baseline (speedup 1.0000x reference)
import contextlib
import numpy as np
import concourse.bass as bass
import concourse.mybir as mybir
from concourse.bass_utils import run_bass_kernel_spmd

F32 = mybir.dt.float32
BF16 = mybir.dt.bfloat16
I32 = mybir.dt.int32
I8 = mybir.dt.int8
AF = mybir.ActivationFunctionType
ALU = mybir.AluOpType
AX = mybir.AxisListType

B, S, D, H, DH, ROTD, C = 2, 8192, 1024, 16, 64, 32, 128
NCH = S // C
OWNCH = 17
OWN = OWNCH * C
OWN_START = [0, 1920, 3968, 6016]
SNAP_AFTER = {14: 0, 30: 1, 46: 2}
NPAIR = 8
GAMMA64 = 1.0 - 2.0 ** (-5.0 - np.arange(16, dtype=np.float64))
NSTAT = float(S * DH)
FFN_GRP = [(0, 768), (768, 768), (1536, 640)]
DBG = 0


def _nsl(n):
    out, o = [], 0
    while o < n:
        w = min(512, n - o)
        out.append((o, w))
        o += w
    return out


def _host_tables():
    t = np.arange(S, dtype=np.float64)
    inv_freq = 1.0 / (10000.0 ** (np.arange(0, ROTD, 2, dtype=np.float64) / ROTD))
    freqs = np.repeat(t[:, None] * inv_freq[None, :], 2, axis=-1)
    base = (np.arange(0, ROTD, 2, dtype=np.float64) + 0.4 * ROTD) / (1.4 * ROTD)
    power = (t - S // 2) / 512.0
    scale = np.repeat(base[None, :] ** power[:, None], 2, axis=-1)
    cos, sin = np.cos(freqs), np.sin(freqs)

    def tile_seq(v):
        W = v.shape[1]
        return np.ascontiguousarray(
            v.reshape(NCH, C, W).transpose(1, 0, 2).reshape(C, NCH * W)).astype(np.float32)

    rot_t = {k: tile_seq(v) for k, v in
             {"qcos": cos * scale, "qsin": sin * scale,
              "kcos": cos / scale, "ksin": sin / scale}.items()}

    g = GAMMA64[:, None]
    with np.errstate(under="ignore"):
        ct = (1.0 - g ** (t[None, :] + 1.0)) / (1.0 - g)
    r_ht = 1.0 / np.sqrt(ct)
    gi = GAMMA64[None, :] ** np.arange(C, dtype=np.float64)[:, None]
    qtab = r_ht.T.reshape(NCH, C, H) * gi[None, :, :]
    qtab_t = np.ascontiguousarray(qtab.transpose(1, 0, 2).reshape(C, NCH * H)).astype(np.float32)

    ginv = GAMMA64[None, :] ** (-np.arange(C, dtype=np.float64)[:, None])
    k1tab = np.repeat(ginv, DH, axis=1).astype(np.float32)
    ktabf = np.zeros((C, NPAIR * C), np.float64)
    for j in range(NPAIR):
        ktabf[0:64, j * C:(j + 1) * C] = ginv[:, 2 * j][None, :]
        ktabf[64:128, j * C:(j + 1) * C] = ginv[:, 2 * j + 1][None, :]
    ktabf = ktabf.astype(np.float32)

    g128 = GAMMA64 ** 128.0
    sdec = np.zeros((C, NPAIR * DH), np.float64)
    for j in range(NPAIR):
        sdec[0:64, j * DH:(j + 1) * DH] = g128[2 * j]
        sdec[64:128, j * DH:(j + 1) * DH] = g128[2 * j + 1]

    cm = (np.arange(C)[None, :] >= np.arange(C)[:, None]).astype(np.float32)
    ind2 = np.zeros((C, 2), np.float32)
    ind2[0:64, 0] = 1.0
    ind2[64:128, 1] = 1.0
    jj = np.arange(C)[:, None]
    ii = np.arange(64)[None, :]
    am_reg = np.where(jj <= 64 + ii, 0.0, -1e30).astype(np.float32)
    am_t0b0 = am_reg.copy()
    am_t0b0[0:64, :] = -1e30
    swp = lambda a: np.ascontiguousarray(np.concatenate([a[64:], a[:64]], axis=0))
    am_b0 = swp(am_reg)
    am_b0t = swp(am_t0b0)
    return dict(rot_t=rot_t, qtab=qtab_t, k1tab=k1tab, ktabf=ktabf,
                sdec=sdec.astype(np.float32), cm=cm, ind2=ind2,
                indT2=np.ascontiguousarray(ind2.T),
                am_b0=am_b0, am_b0t=am_b0t,
                ones1=np.ones((1, C), np.float32),
                ident=np.eye(C, dtype=np.float32),
                am_reg=am_reg, am_t0b0=am_t0b0)


class Sem:
    def __init__(self, h):
        self.h = h
        self.n = 0

    def inc(self, k=1):
        self.n += k
        return self.n


def build(debug=0):
    nc = bass.Bass("TRN2", target_bir_lowering=False)
    dt = nc.dram_tensor

    xT_seq = dt("xT_seq", [NCH, 8, C, C], F32, kind="ExternalInput").ap()
    x_own = dt("x_own", [OWNCH, C, D], F32, kind="ExternalInput").ap()
    xT_own = dt("xT_own", [OWNCH, 8, C, C], F32, kind="ExternalInput").ap()

    wq = dt("l0_wq", [8, C, D], F32, kind="ExternalInput").ap()
    wk = dt("l0_wk", [8, C, D], F32, kind="ExternalInput").ap()
    wv = dt("l0_wv", [8, C, D], F32, kind="ExternalInput").ap()
    wgi = dt("l0_wg", [8, C, D], F32, kind="ExternalInput").ap()
    woi = dt("l0_wo", [8, C, D], F32, kind="ExternalInput").ap()
    bqr = dt("l0_bq", [1, D], F32, kind="ExternalInput").ap()
    bkr = dt("l0_bk", [1, D], F32, kind="ExternalInput").ap()
    bvr = dt("l0_bv", [1, D], F32, kind="ExternalInput").ap()
    bgc_i = dt("l0_bgc", [C, 8], F32, kind="ExternalInput").ap()
    bor = dt("l0_bo", [1, D], F32, kind="ExternalInput").ap()
    gnwb = dt("l0_gnwb", [2, 16], F32, kind="ExternalInput").ap()

    fw1 = dt("f_w1", [8, 8, C, 512], F32, kind="ExternalInput").ap()
    fw2 = dt("f_w2", [8, 8, C, 512], F32, kind="ExternalInput").ap()
    fh1 = dt("f_h1", [32, C, 32, C], F32, kind="ExternalInput").ap()
    fh2 = dt("f_h2", [32, C, 32, C], F32, kind="ExternalInput").ap()
    fw3 = dt("f_w3", [32, C, D], F32, kind="ExternalInput").ap()
    fs1 = dt("f_s1", [C, 32], F32, kind="ExternalInput").ap()
    fs2 = dt("f_s2", [C, 32], F32, kind="ExternalInput").ap()
    fsh1 = dt("f_sh1", [C, 32], F32, kind="ExternalInput").ap()
    fsh2 = dt("f_sh2", [C, 32], F32, kind="ExternalInput").ap()
    fs3r = dt("f_s3", [1, D], F32, kind="ExternalInput").ap()

    awq = dt("l1_wq", [8, C, D], F32, kind="ExternalInput").ap()
    awk = dt("l1_wk", [8, C, D], F32, kind="ExternalInput").ap()
    awv = dt("l1_wv", [8, C, D], F32, kind="ExternalInput").ap()
    awo = dt("l1_wo", [8, C, D], F32, kind="ExternalInput").ap()
    abqc = dt("l1_bqc", [C, 8], F32, kind="ExternalInput").ap()
    abkc = dt("l1_bkc", [C, 8], F32, kind="ExternalInput").ap()
    abvr = dt("l1_bv", [1, D], F32, kind="ExternalInput").ap()
    abor = dt("l1_bo", [1, D], F32, kind="ExternalInput").ap()

    qwq = dt("q_w", [8, C, D], I8, kind="ExternalInput").ap()
    qwsr = dt("q_ws", [1, D], F32, kind="ExternalInput").ap()
    qbr = dt("q_b", [1, D], F32, kind="ExternalInput").ap()

    t_qcos = dt("t_qcos", [C, NCH * ROTD], F32, kind="ExternalInput").ap()
    t_qsin = dt("t_qsin", [C, NCH * ROTD], F32, kind="ExternalInput").ap()
    t_kcos = dt("t_kcos", [C, NCH * ROTD], F32, kind="ExternalInput").ap()
    t_ksin = dt("t_ksin", [C, NCH * ROTD], F32, kind="ExternalInput").ap()
    t_qtab = dt("t_qtab", [C, NCH * H], F32, kind="ExternalInput").ap()
    o_qcos = dt("o_qcos", [C, OWNCH * ROTD], F32, kind="ExternalInput").ap()
    o_qsin = dt("o_qsin", [C, OWNCH * ROTD], F32, kind="ExternalInput").ap()
    o_kcos = dt("o_kcos", [C, OWNCH * ROTD], F32, kind="ExternalInput").ap()
    o_ksin = dt("o_ksin", [C, OWNCH * ROTD], F32, kind="ExternalInput").ap()
    o_qtab = dt("o_qtab", [C, OWNCH * H], F32, kind="ExternalInput").ap()
    t_k1 = dt("t_k1", [C, D], F32, kind="ExternalInput").ap()
    t_ktf = dt("t_ktf", [C, NPAIR * C], F32, kind="ExternalInput").ap()
    t_sdec = dt("t_sdec", [C, NPAIR * DH], F32, kind="ExternalInput").ap()
    t_cm = dt("t_cm", [C, C], F32, kind="ExternalInput").ap()
    t_ind2 = dt("t_ind2", [C, 2], F32, kind="ExternalInput").ap()
    t_indT2 = dt("t_indT2", [2, C], F32, kind="ExternalInput").ap()
    t_ones1 = dt("t_ones1", [1, C], F32, kind="ExternalInput").ap()
    t_ident = dt("t_ident", [C, C], F32, kind="ExternalInput").ap()
    t_amreg = dt("t_amreg", [C, 64], F32, kind="ExternalInput").ap()
    t_amb0 = dt("t_amb0", [C, 64], F32, kind="ExternalInput").ap()
    t_amb0t = dt("t_amb0t", [C, 64], F32, kind="ExternalInput").ap()
    t_coef = dt("t_coef", [C, 3], F32, kind="ExternalInput").ap()

    park1 = dt("park1", [OWNCH, C, D], F32, kind="Internal").ap()
    xtsc = dt("xtsc", [OWNCH, 8, C, C], BF16, kind="Internal").ap()

    y = dt("y", [OWNCH, C, D], F32, kind="ExternalOutput").ap()
    dbg = {}
    if 1 <= debug <= 4:
        dbg["stats"] = dt("dbg_stats", [2, 16], F32, kind="ExternalOutput").ap()
        dbg["gnpp"] = dt("dbg_gnpp", [C, 16], F32, kind="ExternalOutput").ap()
        dbg["snaps"] = dt("dbg_snaps", [C, 3 * NPAIR * DH], F32, kind="ExternalOutput").ap()
        dbg["ret"] = dt("dbg_ret", [C, NPAIR * OWN], F32, kind="ExternalOutput").ap()

    es = contextlib.ExitStack()
    with es:
        sbt = nc.sbuf_tensor
        pst = nc.psum_tensor

        ident = es.enter_context(sbt("ident", [C, C], F32))
        identb = es.enter_context(sbt("identb", [C, C], BF16))
        ones1 = es.enter_context(sbt("ones1", [1, C], BF16))
        gnpp = es.enter_context(sbt("gnpp", [C, 16], F32))

        _semnames = ("st", "sv", "ss", "s_pret", "s_pre", "s_xb0", "s_xb1",
                     "s_gx", "s_xo0", "s_xo1", "s_pk0", "s_pk1",
                     "s_wfa0", "s_wfa1", "s_wfb0", "s_wfb1", "s_wh0", "s_wh1",
                     "s_w30", "s_w31", "s_xg", "s_wl0", "s_wl1", "s_xl",
                     "s_y0", "s_y1", "s_dbg")
        SM = {n: Sem(es.enter_context(nc.semaphore(n))) for n in _semnames}
        st, sv, ss = SM["st"], SM["sv"], SM["ss"]
        s_pret, s_pre, s_gx, s_xg, s_xl = (SM[k] for k in
            ("s_pret", "s_pre", "s_gx", "s_xg", "s_xl"))
        s_xb = [SM["s_xb0"], SM["s_xb1"]]
        s_xo = [SM["s_xo0"], SM["s_xo1"]]
        s_pk = [SM["s_pk0"], SM["s_pk1"]]
        s_wfa = [SM["s_wfa0"], SM["s_wfa1"]]
        s_wfb = [SM["s_wfb0"], SM["s_wfb1"]]
        s_wh = [SM["s_wh0"], SM["s_wh1"]]
        s_w3 = [SM["s_w30"], SM["s_w31"]]
        s_wl = [SM["s_wl0"], SM["s_wl1"]]
        s_y = [SM["s_y0"], SM["s_y1"]]
        s_dbg = SM["s_dbg"]

        block = es.enter_context(nc.Block())

        _AD_OPS = frozenset((
            "tensor_tensor", "tensor_scalar", "tensor_copy", "tensor_add",
            "tensor_sub", "tensor_mul", "tensor_max", "tensor_reduce",
            "reduce_max", "reduce_sum", "activation", "reciprocal", "memset",
            "memzero", "scalar_tensor_tensor", "tensor_single_scalar",
            "tensor_scalar_add", "tensor_scalar_sub", "tensor_scalar_mul",
            "tensor_scalar_max", "tensor_scalar_min", "square", "sqrt",
            "copy", "mul", "add", "select"))

        class _AD:

            def __init__(self, e):
                self._e = e

            def __getattr__(self, k):
                v = getattr(self._e, k)
                if k in _AD_OPS:
                    def w(*a, **kw):
                        r = v(*a, **kw)
                        self._e.drain()
                        return r
                    return w
                return v

        def T(f):
            block.tensor(f)

        def V(f):
            block.vector(lambda e: f(_AD(e)))

        def SC(f):
            block.scalar(lambda e: f(_AD(e)))

        def SY(f):
            block.sync(f)

        def G(f):
            block.gpsimd(lambda e: f(_AD(e)))

        def dma(q, dst, src, sem):
            t = sem.inc(16)
            def _(e, d=dst, s=src, h=sem.h):
                e.dma_start(out=d, in_=s).then_inc(h, 16)
            (G if q == "g" else SY)(_)
            return (sem, t)

        with contextlib.ExitStack() as esA:
            cmask = esA.enter_context(sbt("cmask", [C, C], F32))
            k1tab = esA.enter_context(sbt("k1tab", [C, D], F32))
            ktabf = esA.enter_context(sbt("ktabf", [C, NPAIR * C], F32))
            sdect = esA.enter_context(sbt("sdect", [C, NPAIR * DH], F32))
            ind2f = esA.enter_context(sbt("ind2f", [C, 2], F32))
            indT2f = esA.enter_context(sbt("indT2f", [2, C], F32))
            coef = esA.enter_context(sbt("coef", [C, 3], F32))
            gnwb_s = esA.enter_context(sbt("gnwb_s", [2, 16], F32))
            ocq = esA.enter_context(sbt("ocq", [C, OWNCH * ROTD], F32))
            osq = esA.enter_context(sbt("osq", [C, OWNCH * ROTD], F32))
            ock = esA.enter_context(sbt("ock", [C, OWNCH * ROTD], F32))
            osk = esA.enter_context(sbt("osk", [C, OWNCH * ROTD], F32))
            oqt = esA.enter_context(sbt("oqt", [C, OWNCH * H], F32))
            w_q = esA.enter_context(sbt("w_q", [C, 8 * D], BF16))
            w_k = esA.enter_context(sbt("w_k", [C, 8 * D], BF16))
            w_v = esA.enter_context(sbt("w_v", [C, 8 * D], BF16))
            b_q = esA.enter_context(sbt("b_q", [1, D], BF16))
            b_k = esA.enter_context(sbt("b_k", [1, D], BF16))
            b_v = esA.enter_context(sbt("b_v", [1, D], BF16))
            b_o = esA.enter_context(sbt("b_o", [1, D], BF16))
            bgc = esA.enter_context(sbt("bgc", [C, 8], F32))
            xtb = [esA.enter_context(sbt(f"xtb{i}", [C, 8 * C], BF16)) for i in range(2)]
            q_tm = [esA.enter_context(sbt(f"q_tm{i}", [C, D], F32)) for i in range(2)]
            k_tm = [esA.enter_context(sbt(f"k_tm{i}", [C, D], F32)) for i in range(2)]
            v_tm = [esA.enter_context(sbt(f"v_tm{i}", [C, D], BF16)) for i in range(2)]
            kp_tm = [esA.enter_context(sbt(f"kp_tm{i}", [C, D], BF16)) for i in range(2)]
            rt1 = esA.enter_context(sbt("rt1", [C, 512], F32))
            rt2 = esA.enter_context(sbt("rt2", [C, 512], F32))
            qT = [esA.enter_context(sbt(f"qT{i}", [C, 8 * C], BF16)) for i in range(2)]
            kT = [esA.enter_context(sbt(f"kT{i}", [C, 8 * C], BF16)) for i in range(2)]
            sTm = esA.enter_context(sbt("sTm", [C, 8 * C], BF16))
            s_run = esA.enter_context(sbt("s_run", [C, NPAIR * DH], F32))
            s_bf = esA.enter_context(sbt("s_bf", [C, NPAIR * DH], BF16))
            snaps = esA.enter_context(sbt("snaps", [C, 3 * NPAIR * DH], BF16))
            sqs = esA.enter_context(sbt("sqs", [C, 512], BF16))
            sum1 = esA.enter_context(sbt("sum1", [C, NPAIR * NCH], F32))
            sum2 = esA.enter_context(sbt("sum2", [C, NPAIR * NCH], F32))
            ret_own = esA.enter_context(sbt("ret_own", [C, NPAIR * OWN], BF16))
            gsb = esA.enter_context(sbt("gsb", [2, 16], F32))
            gtmp = esA.enter_context(sbt("gtmp", [2, 8], F32))
            rbuf = esA.enter_context(sbt("rbuf", [2, 16], F32))
            pg = [esA.enter_context(pst(f"pg{i}", [C, 1024], F32)) for i in range(4)]
            esR = contextlib.ExitStack()
            rcq = esR.enter_context(sbt("rcq", [C, NCH * ROTD], F32))
            rsq = esR.enter_context(sbt("rsq", [C, NCH * ROTD], F32))
            rck = esR.enter_context(sbt("rck", [C, NCH * ROTD], F32))
            rsk = esR.enter_context(sbt("rsk", [C, NCH * ROTD], F32))
            qtabs = esR.enter_context(sbt("qtabs", [C, NCH * H], F32))

            for dst, src in ((ident[:], t_ident), (cmask[:], t_cm),
                             (k1tab[:], t_k1), (ktabf[:], t_ktf), (sdect[:], t_sdec),
                             (ind2f[:], t_ind2), (indT2f[:], t_indT2),
                             (coef[:], t_coef),
                             (gnwb_s[:], gnwb), (bgc[:], bgc_i),
                             (rcq[:], t_qcos), (rsq[:], t_qsin),
                             (rck[:], t_kcos), (rsk[:], t_ksin), (qtabs[:], t_qtab),
                             (ocq[:], o_qcos), (osq[:], o_qsin),
                             (ock[:], o_kcos), (osk[:], o_ksin), (oqt[:], o_qtab)):
                dma("y", dst, src, s_pret)
            v3 = lambda t: t[:].rearrange("p (k d) -> p k d", k=8)
            for dst, src in ((v3(w_q), wq.rearrange("k p d -> p k d")),
                             (v3(w_k), wk.rearrange("k p d -> p k d")),
                             (v3(w_v), wv.rearrange("k p d -> p k d")),
                             (b_q[:], bqr), (b_k[:], bkr), (b_v[:], bvr),
                             (b_o[:], bor), (ones1[:], t_ones1),
                             (identb[:], t_ident)):
                dma("g", dst, src, s_pre)
            pre_x, pre_w = (s_pret, s_pret.n), (s_pre, s_pre.n)

            def _(e):
                e.memzero(s_run[:])
                e.memzero(s_bf[:])
                e.memzero(snaps[:])
                e.drain().then_inc(sv.h, 1)
            V(_)
            t_ms = sv.inc(1)

            gd = {"sTm_rd": (st, 0), "sqs_rd": (sv, 0), "s_sup": (sv, 0),
                  "bufs": [(st, 0), (st, 0)], "P1": [(st, 0), (st, 0)],
                  "P2": [(st, 0), (st, 0)]}

            def wt(e, tag):
                if isinstance(tag, str):
                    tag = gd[tag]
                sem, val = tag
                if val > 0:
                    e.wait_ge(sem.h, val)

            def ret_chunk(ci, cglob, xt_src, rot, qtab_sb, stats, ret_col=None,
                          s_init=False, first=False):
                cap = debug - 20 if 20 < debug < 30 else (
                    7 if 30 < debug < 40 else 99)
                sub = debug - 30 if 30 < debug < 40 else 99
                par = ci % 2
                P1, P2 = pg[2 * par], pg[2 * par + 1]
                xb, qm, km, vm, kpm = xtb[par], q_tm[par], k_tm[par], v_tm[par], kp_tm[par]
                qt, kt = qT[par], kT[par]
                rq_c, rq_s, rk_c, rk_s = rot
                buf_guard = gd["bufs"][par]
                p1_guard, p2_guard = gd["P1"][par], gd["P2"][par]

                def _(e):
                    wt(e, buf_guard)
                G(_)
                t_x = dma("g", xb[:].rearrange("p (k t) -> p k t", k=8),
                          xt_src.rearrange("k p t -> p k t"), s_xb[par])
                if cap < 2:
                    return

                def proj(e, dstp, wsb, brow):
                    for hh in range(2):
                        o = dstp[:, hh * 512:hh * 512 + 512]
                        for kk in range(8):
                            e.matmul(out=o, lhsT=xb[:, kk * C:(kk + 1) * C],
                                     rhs=wsb[:, kk * D + hh * 512: kk * D + hh * 512 + 512],
                                     start=(kk == 0), stop=False)
                        e.matmul(out=o, lhsT=ones1[:],
                                 rhs=brow[:, hh * 512:hh * 512 + 512],
                                 start=False, stop=True)

                def _(e):
                    wt(e, t_x)
                    if first:
                        wt(e, pre_x)
                        wt(e, pre_w)
                        e.wait_ge(sv.h, t_ms)
                    wt(e, p1_guard)
                    wt(e, p2_guard)
                    proj(e, P1, w_q, b_q)
                    e.drain().then_inc(st.h, 1)
                    proj(e, P2, w_k, b_k)
                    e.drain().then_inc(st.h, 1)
                T(_)
                t_qmm, t_kmm = st.inc(1), st.inc(1)
                if cap < 3:
                    return

                def _(e):
                    e.wait_ge(st.h, t_qmm)
                    e.activation(out=qm[:], in_=P1[:], func=AF.Copy)
                    e.drain().then_inc(ss.h, 1)
                    e.wait_ge(st.h, t_kmm)
                    e.activation(out=km[:], in_=P2[:], func=AF.Copy)
                    e.drain().then_inc(ss.h, 1)
                SC(_)
                t_qe, t_ke = ss.inc(1), ss.inc(1)
                if cap < 4:
                    return

                def _(e):
                    e.wait_ge(ss.h, t_qe)
                    proj(e, P1, w_v, b_v)
                    e.drain().then_inc(st.h, 1)
                T(_)
                t_vmm = st.inc(1)

                def _(e):
                    e.wait_ge(st.h, t_vmm)
                    e.activation(out=vm[:], in_=P1[:], func=AF.Copy)
                    e.drain().then_inc(ss.h, 1)
                SC(_)
                t_ve = ss.inc(1)
                if cap < 5:
                    return

                def rotary(e, dst, cos_t, sin_t):
                    full = dst[:].rearrange("p (h d) -> p h d", h=H)[:, :, 0:ROTD]
                    cosv = cos_t.rearrange("p (a d) -> p a d", a=1).broadcast_to([C, H, ROTD])
                    sinv = sin_t.rearrange("p (a d) -> p a d", a=1).broadcast_to([C, H, ROTD])
                    t1v = rt1[:].rearrange("p (h d) -> p h d", h=H)
                    t2v = rt2[:].rearrange("p (h d) -> p h d", h=H)
                    ev = lambda ap: ap[:, :, 0:ROTD:2]
                    od = lambda ap: ap[:, :, 1:ROTD:2]
                    e.tensor_tensor(out=t1v, in0=full, in1=cosv, op=ALU.mult)
                    e.tensor_tensor(out=ev(t2v), in0=od(full), in1=ev(sinv), op=ALU.mult)
                    e.tensor_tensor(out=od(t2v), in0=ev(full), in1=od(sinv), op=ALU.mult)
                    e.tensor_sub(out=ev(full), in0=ev(t1v), in1=ev(t2v))
                    e.tensor_add(out=od(full), in0=od(t1v), in1=od(t2v))

                def _(e):
                    e.wait_ge(ss.h, t_qe)
                    rotary(e, qm, rq_c[:, cglob * ROTD:(cglob + 1) * ROTD],
                           rq_s[:, cglob * ROTD:(cglob + 1) * ROTD])
                    qv = qm[:].rearrange("p (h d) -> p h d", h=H)
                    qtv = qtab_sb[:, cglob * H:(cglob + 1) * H].rearrange(
                        "p (h d) -> p h d", d=1).broadcast_to([C, H, DH])
                    e.tensor_tensor(out=qv, in0=qv, in1=qtv, op=ALU.mult)
                    e.drain().then_inc(sv.h, 1)
                    e.wait_ge(ss.h, t_ke)
                    rotary(e, km, rk_c[:, cglob * ROTD:(cglob + 1) * ROTD],
                           rk_s[:, cglob * ROTD:(cglob + 1) * ROTD])
                    e.tensor_tensor(out=kpm[:], in0=km[:], in1=k1tab[:], op=ALU.mult)
                    e.drain().then_inc(sv.h, 1)
                V(_)
                t_qr, t_kr = sv.inc(1), sv.inc(1)
                if cap < 6:
                    return

                if s_init:
                    def _(e):
                        e.tensor_scalar(out=s_run[:], in0=snaps[:, 0:512],
                                        scalar1=coef[:, 0:1], scalar2=None, op0=ALU.mult)
                        for idx in (1, 2):
                            e.scalar_tensor_tensor(
                                out=s_run[:], in0=snaps[:, idx * 512:(idx + 1) * 512],
                                scalar=coef[:, idx:idx + 1], in1=s_run[:],
                                op0=ALU.mult, op1=ALU.add)
                        e.tensor_copy(out=s_bf[:], in_=s_run[:])
                        e.drain().then_inc(sv.h, 1)
                    V(_)
                    gd["s_sup"] = (sv, sv.inc(1))

                def _(e):
                    e.wait_ge(sv.h, t_qr)
                    e.wait_ge(ss.h, t_ke)
                    for kk in range(8):
                        e.transpose(out=P2[:, kk * C:(kk + 1) * C],
                                    in_=qm[:, kk * C:(kk + 1) * C], identity=ident[:])
                    e.drain().then_inc(st.h, 1)
                    e.wait_ge(sv.h, t_kr)
                    e.wait_ge(ss.h, t_ve)
                    for kk in range(8):
                        e.transpose(out=P1[:, kk * C:(kk + 1) * C],
                                    in_=km[:, kk * C:(kk + 1) * C], identity=ident[:])
                    e.drain().then_inc(st.h, 1)
                T(_)
                t_qT, t_kT = st.inc(1), st.inc(1)

                def _(e):
                    e.wait_ge(st.h, t_qT)
                    e.activation(out=qt[:], in_=P2[:], func=AF.Copy)
                    e.drain().then_inc(ss.h, 1)
                SC(_)
                t_qTe = ss.inc(1)

                def _(e):
                    e.wait_ge(st.h, t_kT)
                    e.tensor_tensor(out=kt[:], in0=P1[:], in1=ktabf[:], op=ALU.mult)
                    e.drain().then_inc(sv.h, 1)
                V(_)
                t_kTe = ss.n
                t_kTe_v = sv.inc(1)
                if cap < 7:
                    return

                sup_guard = gd["s_sup"]
                t_m_prev = None
                p1_readers = []
                for rnd in range(2):
                    def _(e, rnd=rnd):
                        e.wait_ge(ss.h, t_qTe)
                        e.wait_ge(sv.h, t_kTe_v)
                        if rnd == 1 and t_m_prev is not None:
                            e.wait_ge(sv.h, t_m_prev)
                        for hh in range(2):
                            r0 = hh * 64
                            if hh == 1:
                                e.drain()
                            for jj4 in range(4):
                                j = rnd * 4 + jj4
                                e.matmul(out=P2[:, (jj4 * 2 + hh) * C:(jj4 * 2 + hh + 1) * C],
                                         lhsT=kt[r0:r0 + 64, j * C:(j + 1) * C],
                                         rhs=qt[r0:r0 + 64, j * C:(j + 1) * C],
                                         start=True, stop=True,
                                         tile_position=(r0, 0))
                        e.drain().then_inc(st.h, 1)
                    T(_)
                    t_sc = st.inc(1)
                    if sub < 2:
                        continue

                    def _(e, t_sc=t_sc):
                        e.wait_ge(st.h, t_sc)
                        wt(e, gd["sTm_rd"])
                        e.tensor_tensor(
                            out=sTm[:].rearrange("p (s t) -> p s t", s=8),
                            in0=P2[:].rearrange("p (s t) -> p s t", s=8),
                            in1=cmask[:].rearrange("p (a t) -> p a t", a=1).broadcast_to([C, 8, C]),
                            op=ALU.mult)
                        e.drain().then_inc(sv.h, 1)
                    V(_)
                    t_m = sv.inc(1)
                    t_m_prev = t_m
                    if sub < 3:
                        continue

                    def _(e, rnd=rnd, t_m=t_m):
                        e.wait_ge(sv.h, t_m)
                        wt(e, sup_guard)
                        for jj4 in range(4):
                            j = rnd * 4 + jj4
                            oc = jj4 * C
                            for hh in range(2):
                                r0 = hh * 64
                                e.matmul(out=P1[r0:r0 + 64, oc:oc + C],
                                         lhsT=vm[:, j * C + hh * 64: j * C + hh * 64 + 64],
                                         rhs=sTm[:, (jj4 * 2 + hh) * C:(jj4 * 2 + hh + 1) * C],
                                         start=True, stop=False, tile_position=(0, r0))
                                e.matmul(out=P1[r0:r0 + 64, oc:oc + C],
                                         lhsT=s_bf[r0:r0 + 64, j * DH:(j + 1) * DH],
                                         rhs=qt[r0:r0 + 64, j * C:(j + 1) * C],
                                         start=False, stop=True, tile_position=(r0, r0))
                        e.drain().then_inc(st.h, 1)
                    T(_)
                    t_ret = st.inc(1)
                    gd["sTm_rd"] = (st, t_ret)

                    if stats and sub < 4:
                        p1_readers = [(st, t_ret)]
                    elif stats:
                        def _(e, rnd=rnd, t_ret=t_ret, cglob=cglob):
                            e.wait_ge(st.h, t_ret)
                            dst = sum1[:].rearrange("p (j c) -> p j c", j=NPAIR)[
                                :, rnd * 4:rnd * 4 + 4, cglob:cglob + 1]
                            e.tensor_reduce(out=dst,
                                            in_=P1[:, 0:512].rearrange("p (s t) -> p s t", s=4),
                                            axis=AX.X, op=ALU.add)
                            e.drain().then_inc(sv.h, 1)
                        V(_)
                        t_r1 = sv.inc(1)

                        def _(e, t_ret=t_ret):
                            e.wait_ge(st.h, t_ret)
                            wt(e, gd["sqs_rd"])
                            e.activation(out=sqs[:], in_=P1[:, 0:512], func=AF.Square)
                            e.drain().then_inc(ss.h, 1)
                        SC(_)
                        t_sq = ss.inc(1)

                        def _(e, rnd=rnd, cglob=cglob, t_sq=t_sq):
                            e.wait_ge(ss.h, t_sq)
                            dstv = sum2[:].rearrange("p (j c) -> p j c", j=NPAIR)
                            e.tensor_reduce(
                                out=dstv[:, rnd * 4:rnd * 4 + 4, cglob:cglob + 1],
                                in_=sqs[:].rearrange("p (s t) -> p s t", s=4),
                                axis=AX.X, op=ALU.add)
                            e.drain().then_inc(sv.h, 1)
                        V(_)
                        t_r2 = sv.inc(1)
                        gd["sqs_rd"] = (sv, t_r2)
                        p1_readers = [(sv, t_r1), (ss, t_sq)]
                    else:
                        def _(e, rnd=rnd, t_ret=t_ret, ret_col=ret_col):
                            e.wait_ge(st.h, t_ret)
                            dst = ret_own[:].rearrange("p (j t) -> p j t", j=NPAIR)[
                                :, rnd * 4:rnd * 4 + 4, ret_col:ret_col + C]
                            e.activation(out=dst,
                                         in_=P1[:, 0:512].rearrange("p (s t) -> p s t", s=4),
                                         func=AF.Copy)
                            e.drain().then_inc(ss.h, 1)
                        SC(_)
                        p1_readers = [(ss, ss.inc(1))]

                    if rnd == 0:
                        def _(e, rds=list(p1_readers)):
                            for gdi in rds:
                                wt(e, gdi)
                        T(_)

                if p1_readers:
                    gd["P1"][par] = p1_readers[-1]
                if cap < 8:
                    return

                def _(e, t_m=t_m_prev):
                    e.wait_ge(sv.h, t_m)
                    for j in range(NPAIR):
                        for hh in range(2):
                            r0 = hh * 64
                            e.matmul(out=P2[r0:r0 + 64, j * DH:(j + 1) * DH],
                                     lhsT=kpm[:, j * C + hh * 64: j * C + hh * 64 + 64],
                                     rhs=vm[:, j * C + hh * 64: j * C + hh * 64 + 64],
                                     start=True, stop=True, tile_position=(0, r0))
                    e.drain().then_inc(st.h, 1)
                T(_)
                t_stm = st.inc(1)
                gd["bufs"][par] = (st, t_stm)

                def _(e, cglob=cglob, stats=stats):
                    e.wait_ge(st.h, t_stm)
                    e.tensor_tensor(out=s_run[:], in0=P2[:, 0:NPAIR * DH],
                                    in1=s_run[:], op=ALU.add)
                    e.tensor_tensor(out=s_run[:], in0=s_run[:], in1=sdect[:], op=ALU.mult)
                    e.tensor_copy(out=s_bf[:], in_=s_run[:])
                    if stats and cglob in SNAP_AFTER:
                        si = SNAP_AFTER[cglob]
                        e.tensor_copy(out=snaps[:, si * 512:(si + 1) * 512], in_=s_run[:])
                    e.drain().then_inc(sv.h, 1)
                V(_)
                t_sup = sv.inc(1)
                gd["s_sup"] = (sv, t_sup)
                gd["P2"][par] = (sv, t_sup)

            p1n = {7: 0, 8: 8, 9: 1, 10: 2}.get(
                debug, 1 if 20 < debug < 40 else NCH)
            for cg in range(p1n):
                ret_chunk(cg, cg, xT_seq[cg], (rcq, rsq, rck, rsk), qtabs,
                          stats=True, first=(cg == 0))

            def _(e):
                wt(e, gd["s_sup"])
                e.tensor_reduce(out=gnpp[:, 0:8],
                                in_=sum1[:].rearrange("p (j c) -> p j c", j=NPAIR),
                                axis=AX.X, op=ALU.add)
                e.tensor_reduce(out=gnpp[:, 8:16],
                                in_=sum2[:].rearrange("p (j c) -> p j c", j=NPAIR),
                                axis=AX.X, op=ALU.add)
                e.drain().then_inc(sv.h, 1)
            V(_)
            t_tot = sv.inc(1)

            def _(e):
                e.wait_ge(sv.h, t_tot)
                wt(e, gd["P1"][0])
                e.matmul(out=pg[0][0:2, 0:16], lhsT=ind2f[:], rhs=gnpp[:, 0:16],
                         start=True, stop=True)
                e.drain().then_inc(st.h, 1)
            T(_)
            t_hs = st.inc(1)

            def _(e):
                e.wait_ge(st.h, t_hs)
                e.tensor_copy(out=gsb[:], in_=pg[0][0:2, 0:16])
                e.tensor_scalar(out=gsb[:], in0=gsb[:], scalar1=1.0 / NSTAT,
                                scalar2=None, op0=ALU.mult)
                e.tensor_tensor(out=gtmp[:], in0=gsb[0:2, 0:8], in1=gsb[0:2, 0:8], op=ALU.mult)
                e.tensor_tensor(out=gsb[0:2, 8:16], in0=gsb[0:2, 8:16], in1=gtmp[:], op=ALU.subtract)
                e.tensor_scalar(out=gsb[0:2, 8:16], in0=gsb[0:2, 8:16], scalar1=1e-5,
                                scalar2=None, op0=ALU.add)
                e.drain().then_inc(sv.h, 1)
            V(_)
            t_g1 = sv.inc(1)
            gd["P1"][0] = (sv, t_g1)

            def _(e):
                e.wait_ge(sv.h, t_g1)
                e.activation(out=gsb[0:2, 8:16], in_=gsb[0:2, 8:16], func=AF.Sqrt)
                e.drain().then_inc(ss.h, 1)
            SC(_)
            t_g2 = ss.inc(1)

            def _(e):
                e.wait_ge(ss.h, t_g2)
                e.reciprocal(out=gsb[0:2, 8:16], in_=gsb[0:2, 8:16])
                e.tensor_tensor(out=gtmp[:], in0=gsb[0:2, 8:16], in1=gnwb_s[0:2, 0:8], op=ALU.mult)
                e.tensor_tensor(out=gsb[0:2, 0:8], in0=gsb[0:2, 0:8], in1=gtmp[:], op=ALU.mult)
                e.tensor_tensor(out=gsb[0:2, 0:8], in0=gnwb_s[0:2, 8:16], in1=gsb[0:2, 0:8], op=ALU.subtract)
                e.tensor_copy(out=rbuf[0:2, 0:16:2], in_=gtmp[:])
                e.tensor_copy(out=rbuf[0:2, 1:16:2], in_=gsb[0:2, 0:8])
                e.drain().then_inc(sv.h, 1)
            V(_)
            t_g3 = sv.inc(1)

            def _(e):
                e.wait_ge(sv.h, t_g3)
                wt(e, gd["P2"][0])
                e.matmul(out=pg[1][:, 0:16], lhsT=indT2f[:], rhs=rbuf[:],
                         start=True, stop=True)
                e.drain().then_inc(st.h, 1)
            T(_)
            t_g4 = st.inc(1)

            def _(e):
                e.wait_ge(st.h, t_g4)
                e.tensor_copy(out=gnpp[:], in_=pg[1][:, 0:16])
                e.drain().then_inc(sv.h, 1)
            V(_)
            t_gn = sv.inc(1)
            gd["P2"][0] = (sv, t_gn)

            if 1 <= debug <= 4:
                def _(e):
                    e.wait_ge(sv.h, t_gn)
                SY(_)
                dma("y", dbg["gnpp"], gnpp[:], s_pret)
                dma("y", dbg["stats"], gsb[:], s_pret)

            esR.close()

            if debug in (6, 7, 8, 9, 10) or 20 < debug < 40:
                def _(e, t_gn=t_gn):
                    e.wait_ge(sv.h, t_gn)
                    for t in range(OWNCH):
                        for j in range(8):
                            e.dma_start(out=y[t][:, j * C:(j + 1) * C],
                                        in_=ident[:]).then_inc(s_y[0].h, 16)
                    e.wait_ge(s_y[0].h, s_y[0].inc(16 * OWNCH * 8))
                SY(_)
                return nc
            esG_gate = [(st, st.n), (sv, sv.n), (ss, ss.n)]
            esG = contextlib.ExitStack()
            gate_sb = esG.enter_context(sbt("gate_sb", [C, 1024], BF16))
            gxt = esG.enter_context(sbt("gxt", [C, 8 * 512], BF16))
            xo_sb = [esG.enter_context(sbt(f"xo_sb{i}", [C, D], F32)) for i in range(2)]
            strm = [esG.enter_context(sbt(f"strm{i}", [C, D], F32)) for i in range(2)]
            sqd = esG.enter_context(sbt("sqd", [C, D], BF16))
            lnt = esG.enter_context(sbt("lnt", [C, 8], F32))
            xt_t = esG.enter_context(sbt("xt_t", [C, 8 * C], BF16))

            dbg_gate = (s_dbg, 0)

            for oc in range(OWNCH):
                ret_chunk(NCH + oc, oc, xT_own[oc], (ocq, osq, ock, osk), oqt,
                          stats=False, ret_col=oc * C, s_init=(oc == 0))

            if 1 <= debug <= 4:
                def _(e):
                    wt(e, gd["P1"][0])
                    wt(e, gd["P1"][1])
                    wt(e, gd["s_sup"])
                    e.drain().then_inc(sv.h, 1)
                V(_)
                t_dj = sv.inc(1)
                def _(e, t_dj=t_dj, a=ss.n):
                    e.wait_ge(sv.h, t_dj)
                    e.wait_ge(ss.h, a)
                G(_)
                dma("g", dbg["snaps"], snaps[:], s_dbg)
                dma("g", dbg["ret"], ret_own[:], s_dbg)
            dbg_gate = (s_dbg, s_dbg.n)

            def _(e, a=st.n):
                e.wait_ge(st.h, a)
            G(_)
            t_wgl = dma("g", w_q[:].rearrange("p (k d) -> p k d", k=8),
                        wgi.rearrange("k p d -> p k d"), s_pre)
            t_wol = dma("g", w_k[:].rearrange("p (k d) -> p k d", k=8),
                        woi.rearrange("k p d -> p k d"), s_pre)
            t_wgl = t_wol
            w_g, w_o = w_q, w_k
            gate_guard = gd["P2"][0]
            sil_prev = (ss, 0)
            mod_prev = (sv, 0)
            gxt_guard = (st, 0)
            first_g = True
            for (tok0, tw) in _nsl(OWN):
                oc0, noc = tok0 // C, (tw + C - 1) // C
                def _(e, g_=gxt_guard, fg=first_g):
                    if fg:
                        for t_ in esG_gate:
                            wt(e, t_)
                    wt(e, g_)
                G(_)
                first_g = False
                for oi in range(noc):
                    t_gx = dma("g",
                               gxt[:, 0:8 * tw].rearrange("p (k t) -> p k t", k=8)[
                                   :, :, oi * C:(oi + 1) * C],
                               xT_own[oc0 + oi].rearrange("k p t -> p k t"), s_gx)
                for j in range(NPAIR):
                    half = (j % 2) * 512
                    def _(e, j=j, tw=tw, t_gx=t_gx, half=half, gg=gate_guard, sp=sil_prev):
                        wt(e, t_gx)
                        wt(e, t_wgl)
                        wt(e, gg)
                        wt(e, sp)
                        for kk in range(8):
                            e.matmul(out=pg[1][:, half:half + tw],
                                     lhsT=w_g[:, kk * D + j * C: kk * D + j * C + C],
                                     rhs=gxt[:, kk * tw:(kk + 1) * tw],
                                     start=(kk == 0), stop=(kk == 7))
                        e.drain().then_inc(st.h, 1)
                    T(_)
                    t_gmm = st.inc(1)
                    gxt_guard = (st, t_gmm)
                    gate_guard = (st, 0)

                    def _(e, j=j, tw=tw, half=half, t_gmm=t_gmm, mp=mod_prev):
                        e.wait_ge(st.h, t_gmm)
                        wt(e, mp)
                        e.activation(out=gate_sb[:, half:half + tw],
                                     in_=pg[1][:, half:half + tw],
                                     func=AF.Silu, bias=bgc[:, j:j + 1])
                        e.drain().then_inc(ss.h, 1)
                    SC(_)
                    sil_prev = (ss, ss.inc(1))

                    def _(e, j=j, tok0=tok0, tw=tw, half=half, ts_=sil_prev):
                        wt(e, ts_)
                        wt(e, dbg_gate)
                        e.wait_ge(sv.h, t_gn)
                        sl = ret_own[:, j * OWN + tok0: j * OWN + tok0 + tw]
                        e.tensor_scalar(out=sl, in0=sl, scalar1=gnpp[:, 2 * j:2 * j + 1],
                                        scalar2=gnpp[:, 2 * j + 1:2 * j + 2],
                                        op0=ALU.mult, op1=ALU.add)
                        e.tensor_tensor(out=sl, in0=sl, in1=gate_sb[:, half:half + tw],
                                        op=ALU.mult)
                        e.drain().then_inc(sv.h, 1)
                    V(_)
                    mod_prev = (sv, sv.inc(1))
            t_mod_all = mod_prev


            def boundary_ln(psumo, tile, t_pmm, t_xo, xo, st_, spk, final=False):
                def _(e, t_pmm=t_pmm, t_xo=t_xo):
                    e.wait_ge(st.h, t_pmm)
                    wt(e, t_xo)
                    e.tensor_tensor(out=st_[:], in0=psumo[:], in1=xo[:], op=ALU.add)
                    e.tensor_reduce(out=lnt[:, 0:1], in_=st_[:], axis=AX.X, op=ALU.add)
                    e.drain().then_inc(sv.h, 1)
                V(_)
                t_b1 = sv.inc(1)

                def _(e, t_b1=t_b1):
                    e.wait_ge(sv.h, t_b1)
                    e.activation(out=sqd[:], in_=st_[:], func=AF.Square)
                    e.drain().then_inc(ss.h, 1)
                SC(_)
                t_b2 = ss.inc(1)

                def _(e, t_b2=t_b2):
                    e.wait_ge(ss.h, t_b2)
                    e.tensor_reduce(out=lnt[:, 1:2], in_=sqd[:], axis=AX.X, op=ALU.add)
                    e.tensor_scalar(out=lnt[:, 0:2], in0=lnt[:, 0:2], scalar1=1.0 / D,
                                    scalar2=None, op0=ALU.mult)
                    e.tensor_tensor(out=lnt[:, 2:3], in0=lnt[:, 0:1], in1=lnt[:, 0:1], op=ALU.mult)
                    e.tensor_tensor(out=lnt[:, 1:2], in0=lnt[:, 1:2], in1=lnt[:, 2:3], op=ALU.subtract)
                    e.tensor_scalar(out=lnt[:, 1:2], in0=lnt[:, 1:2], scalar1=1e-5,
                                    scalar2=None, op0=ALU.add)
                    e.drain().then_inc(sv.h, 1)
                V(_)
                t_b3 = sv.inc(1)

                def _(e, t_b3=t_b3):
                    e.wait_ge(sv.h, t_b3)
                    e.activation(out=lnt[:, 1:2], in_=lnt[:, 1:2], func=AF.Sqrt)
                    e.drain().then_inc(ss.h, 1)
                SC(_)
                t_b4 = ss.inc(1)

                def _(e, t_b4=t_b4):
                    e.wait_ge(ss.h, t_b4)
                    e.reciprocal(out=lnt[:, 1:2], in_=lnt[:, 1:2])
                    e.tensor_scalar(out=st_[:], in0=st_[:], scalar1=lnt[:, 0:1],
                                    scalar2=lnt[:, 1:2], op0=ALU.subtract, op1=ALU.mult)
                    e.drain().then_inc(sv.h, 1)
                V(_)
                t_b5 = sv.inc(1)

                def _(e, t_b5=t_b5):
                    e.wait_ge(sv.h, t_b5)
                    wt(e, gd.get("P3", (ss, 0)))
                    for kk in range(8):
                        e.transpose(out=pg[3][:, kk * C:(kk + 1) * C],
                                    in_=st_[:, kk * C:(kk + 1) * C], identity=ident[:])
                    e.drain().then_inc(st.h, 1)
                T(_)
                t_b6 = st.inc(1)

                def _(e, t_b6=t_b6, xg=gd.get("XT_A", (ss, 0))):
                    e.wait_ge(st.h, t_b6)
                    wt(e, xg)
                    e.activation(out=xt_t[:], in_=pg[3][:], func=AF.Copy)
                    e.drain().then_inc(ss.h, 1)
                SC(_)
                t_b7 = ss.inc(1)
                gd["P3"] = (ss, t_b7)

                def _(e, tile=tile, t_b7=t_b7, t_b5=t_b5):
                    e.wait_ge(ss.h, t_b7)
                    e.dma_start(out=xtsc[tile].rearrange("k p t -> p k t"),
                                in_=xt_t[:].rearrange("p (k t) -> p k t", k=8)
                                ).then_inc(spk.h, 16)
                    e.dma_start(out=park1[tile], in_=st_[:]).then_inc(spk.h, 16)
                SY(_)
                spk.inc(32)
                gd["XT_A"] = (spk, spk.n)
                return (spk, spk.n), t_b5

            strm_free = [(sv, 0), (sv, 0)]
            gd["P3"] = gd["P2"][1]
            po_guard = [gd["P1"][0], gd["P1"][1]]
            def _(e):
                for t_ in esG_gate:
                    wt(e, t_)
            SY(_)
            for tile in range(OWNCH):
                par = tile % 2
                PO = pg[2 * par]
                t_xo = dma("y", xo_sb[par][:], x_own[tile], s_xo[par])
                def _(e, tile=tile, par=par, pog=po_guard[par], sf=strm_free[par]):
                    wt(e, t_mod_all)
                    wt(e, t_wol)
                    wt(e, pog)
                    wt(e, sf)
                    for hh in range(2):
                        o = PO[:, hh * 512:hh * 512 + 512]
                        for j in range(NPAIR):
                            e.matmul(out=o,
                                     lhsT=ret_own[:, j * OWN + tile * C: j * OWN + tile * C + C],
                                     rhs=w_o[:, j * D + hh * 512: j * D + hh * 512 + 512],
                                     start=(j == 0), stop=False)
                        e.matmul(out=o, lhsT=ones1[:], rhs=b_o[:, hh * 512:hh * 512 + 512],
                                 start=False, stop=True)
                    e.drain().then_inc(st.h, 1)
                T(_)
                t_pmm = st.inc(1)
                g, t_psfree = boundary_ln(PO, tile, t_pmm, t_xo, xo_sb[par],
                                          strm[par], s_pk[par])
                strm_free[par] = g
                po_guard[par] = (sv, t_psfree)
            tA_parks = [(s_pk[0], s_pk[0].n), (s_pk[1], s_pk[1].n)]
            tA_strm = [strm_free[0], strm_free[1]]
            def _(e, a=st.n, b=ss.n, pk0=s_pk[0].n, pk1=s_pk[1].n):
                e.wait_ge(st.h, a)
                e.wait_ge(ss.h, b)
                e.wait_ge(s_pk[0].h, pk0)
                e.wait_ge(s_pk[1].h, pk1)
                e.drain().then_inc(sv.h, 1)
            V(_)
            tA_bar = (sv, sv.inc(1))
            esG.close()
            if debug in (2, 5):
                def _(e):
                    wt(e, tA_parks[0])
                    wt(e, tA_parks[1])
                    for t in range(OWNCH):
                        e.dma_start(out=y[t], in_=park1[t]).then_inc(s_y[0].h, 16)
                    e.wait_ge(s_y[0].h, s_y[0].inc(16 * OWNCH))
                SY(_)

        if debug in (1, 2, 5):
            def _(e, a=s_pret.n, b=s_dbg.n):
                e.wait_ge(s_pret.h, a)
                e.wait_ge(s_dbg.h, b)
            SY(_)
            return nc

        with contextlib.ExitStack() as esB:
            GT = 768
            h1b = esB.enter_context(sbt("h1b", [C, 32 * GT], BF16))
            h2b = esB.enter_context(sbt("h2b", [C, 32 * GT], BF16))
            xgb = esB.enter_context(sbt("xgb", [C, 8 * GT], BF16))
            wfa = [esB.enter_context(sbt(f"wfa{i}", [C, 8 * 512], BF16)) for i in range(2)]
            wfb = [esB.enter_context(sbt(f"wfb{i}", [C, 8 * 512], BF16)) for i in range(2)]
            whb = [esB.enter_context(sbt(f"whb{i}", [C, 32 * C], BF16)) for i in range(2)]
            w3b = [esB.enter_context(sbt(f"w3b{i}", [C, 4 * D], BF16)) for i in range(2)]
            s1s = esB.enter_context(sbt("s1s", [C, 32], F32))
            s2s = esB.enter_context(sbt("s2s", [C, 32], F32))
            sh1s = esB.enter_context(sbt("sh1s", [C, 32], F32))
            sh2s = esB.enter_context(sbt("sh2s", [C, 32], F32))
            s3b = esB.enter_context(sbt("s3b", [C, D], F32))
            t1f = esB.enter_context(sbt("t1f", [C, 512], BF16))
            xo2 = [esB.enter_context(sbt(f"xo2_{i}", [C, D], F32)) for i in range(2)]
            strm2 = [esB.enter_context(sbt(f"strm2_{i}", [C, D], F32)) for i in range(2)]
            sqd2 = esB.enter_context(sbt("sqd2", [C, D], BF16))
            lnt2 = esB.enter_context(sbt("lnt2", [C, 8], F32))
            xtt2 = esB.enter_context(sbt("xtt2", [C, 8 * C], BF16))
            pB4 = [esB.enter_context(pst(f"pB4_{i}", [C, 1024], F32)) for i in range(4)]

            def _(e):
                wt(e, tA_bar)
            SY(_)
            def _(e):
                wt(e, tA_bar)
            G(_)
            for dst, src_ in ((s1s[:], fs1), (s2s[:], fs2), (sh1s[:], fsh1),
                              (sh2s[:], fsh2)):
                dma("y", dst, src_, s_pret)
            dma("g", s3b[:], fs3r.partition_broadcast(C), s_pre)
            tB_tab = (s_pret, s_pret.n)
            tB_tabw = (s_pre, s_pre.n)

            def lnboundary2(tile, PSUM, t_pmm, t_xo, xo, st_, sqd_, lnt_, xtt_,
                            scale_tab=None, pgu=None):
                def _(e, t_pmm=t_pmm, t_xo=t_xo):
                    e.wait_ge(st.h, t_pmm)
                    wt(e, t_xo)
                    wt(e, tB_tab)
                    wt(e, tB_tabw)
                    if scale_tab is not None:
                        e.tensor_tensor(out=st_[:], in0=PSUM[:], in1=scale_tab[:], op=ALU.mult)
                        e.tensor_tensor(out=st_[:], in0=st_[:], in1=xo[:], op=ALU.add)
                    else:
                        e.tensor_tensor(out=st_[:], in0=PSUM[:], in1=xo[:], op=ALU.add)
                    e.tensor_reduce(out=lnt_[:, 0:1], in_=st_[:], axis=AX.X, op=ALU.add)
                    e.drain().then_inc(sv.h, 1)
                V(_)
                t_b1 = sv.inc(1)

                def _(e, t_b1=t_b1):
                    e.wait_ge(sv.h, t_b1)
                    e.activation(out=sqd_[:], in_=st_[:], func=AF.Square)
                    e.drain().then_inc(ss.h, 1)
                SC(_)
                t_b2 = ss.inc(1)

                def _(e, t_b2=t_b2):
                    e.wait_ge(ss.h, t_b2)
                    e.tensor_reduce(out=lnt_[:, 1:2], in_=sqd_[:], axis=AX.X, op=ALU.add)
                    e.tensor_scalar(out=lnt_[:, 0:2], in0=lnt_[:, 0:2], scalar1=1.0 / D,
                                    scalar2=None, op0=ALU.mult)
                    e.tensor_tensor(out=lnt_[:, 2:3], in0=lnt_[:, 0:1], in1=lnt_[:, 0:1], op=ALU.mult)
                    e.tensor_tensor(out=lnt_[:, 1:2], in0=lnt_[:, 1:2], in1=lnt_[:, 2:3], op=ALU.subtract)
                    e.tensor_scalar(out=lnt_[:, 1:2], in0=lnt_[:, 1:2], scalar1=1e-5,
                                    scalar2=None, op0=ALU.add)
                    e.drain().then_inc(sv.h, 1)
                V(_)
                t_b3 = sv.inc(1)

                def _(e, t_b3=t_b3):
                    e.wait_ge(sv.h, t_b3)
                    e.activation(out=lnt_[:, 1:2], in_=lnt_[:, 1:2], func=AF.Sqrt)
                    e.drain().then_inc(ss.h, 1)
                SC(_)
                t_b4 = ss.inc(1)

                def _(e, t_b4=t_b4):
                    e.wait_ge(ss.h, t_b4)
                    e.reciprocal(out=lnt_[:, 1:2], in_=lnt_[:, 1:2])
                    e.tensor_scalar(out=st_[:], in0=st_[:], scalar1=lnt_[:, 0:1],
                                    scalar2=lnt_[:, 1:2], op0=ALU.subtract, op1=ALU.mult)
                    e.drain().then_inc(sv.h, 1)
                V(_)
                t_b5 = sv.inc(1)

                def _(e, t_b5=t_b5):
                    e.wait_ge(sv.h, t_b5)
                    wt(e, pgu[(3, 0)])
                    wt(e, pgu[(3, 1)])
                    for kk in range(8):
                        e.transpose(out=pB4[3][:, kk * C:(kk + 1) * C],
                                    in_=st_[:, kk * C:(kk + 1) * C], identity=ident[:])
                    e.drain().then_inc(st.h, 1)
                T(_)
                t_b6 = st.inc(1)

                def _(e, t_b6=t_b6, xg=gd.get("XT_B", (ss, 0))):
                    e.wait_ge(st.h, t_b6)
                    wt(e, xg)
                    e.activation(out=xtt_[:], in_=pB4[3][:], func=AF.Copy)
                    e.drain().then_inc(ss.h, 1)
                SC(_)
                t_b7 = ss.inc(1)
                pgu[(3, 0)] = (ss, t_b7)
                pgu[(3, 1)] = (ss, t_b7)

                spk = s_pk[tile % 2]
                def _(e, tile=tile, t_b7=t_b7, t_b5=t_b5, spk=spk):
                    e.wait_ge(ss.h, t_b7)
                    e.dma_start(out=xtsc[tile].rearrange("k p t -> p k t"),
                                in_=xtt_[:].rearrange("p (k t) -> p k t", k=8)
                                ).then_inc(spk.h, 16)
                    e.dma_start(out=park1[tile], in_=st_[:]).then_inc(spk.h, 16)
                SY(_)
                spk.inc(32)
                gd["XT_B"] = (spk, spk.n)
                return (spk, spk.n), t_b5

            pgu = {}
            def pwait(e, ti, hh):
                g = pgu.get((ti, hh))
                if g:
                    wt(e, g)
            for i in range(4):
                for hh in range(2):
                    pgu[(i, hh)] = tA_bar

            h1_rd = (st, 0)
            h2_rd = (st, 0)
            t1_rd = (sv, 0)
            xg_rd = (st, 0)
            wdbl_free = {"a": [(st, 0), (st, 0)], "b": [(st, 0), (st, 0)],
                         "h": [(st, 0), (st, 0)], "w3": [(st, 0), (st, 0)]}
            strm2_free = [(s_pk[0], 0), (s_pk[1], 0)]
            gd["PX"] = (ss, 0)

            for gi_, (gt0, gtw) in enumerate(FFN_GRP):
                nsl = _nsl(gtw)
                ntile = gtw // C
                oc0 = gt0 // C

                def _(e, xg=xg_rd):
                    wt(e, tA_parks[0])
                    wt(e, tA_parks[1])
                    wt(e, xg)
                SY(_)
                for oi in range(ntile):
                    t_xg = dma("y",
                               xgb[:, 0:8 * gtw].rearrange("p (k t) -> p k t", k=8)[
                                   :, :, oi * C:(oi + 1) * C],
                               xtsc[oc0 + oi].rearrange("k p t -> p k t"), s_xg)

                for ob in range(8):
                    par = ob % 2
                    def _(e, g1=wdbl_free["a"][par], g2=wdbl_free["b"][par]):
                        wt(e, g1)
                        wt(e, g2)
                    G(_)
                    t_wa = dma("g", wfa[par][:].rearrange("p (k d) -> p k d", k=8),
                               fw1[ob].rearrange("k p d -> p k d"), s_wfa[par])
                    t_wb = dma("g", wfb[par][:].rearrange("p (k d) -> p k d", k=8),
                               fw2[ob].rearrange("k p d -> p k d"), s_wfb[par])
                    for o2 in range(4):
                        o = ob * 4 + o2
                        hf = o2 % 2
                        for (so_, swd) in nsl:
                            pA = pB4[0][:, hf * 512: hf * 512 + swd]
                            pBp = pB4[1][:, hf * 512: hf * 512 + swd]
                            def _(e, par=par, o2=o2, so_=so_, swd=swd, t_wa=t_wa,
                                  t_wb=t_wb, t_xg=t_xg, pA=pA, pBp=pBp, hf=hf,
                                  g0=pgu[(0, hf)], g1=pgu[(1, hf)], xr=xg_rd, h1r=h1_rd):
                                wt(e, t_wa)
                                wt(e, t_wb)
                                wt(e, t_xg)
                                wt(e, g0)
                                wt(e, g1)
                                wt(e, h1r)
                                for kk in range(8):
                                    e.matmul(out=pA,
                                             lhsT=wfa[par][:, kk * 512 + o2 * C: kk * 512 + o2 * C + C],
                                             rhs=xgb[:, kk * gtw + so_: kk * gtw + so_ + swd],
                                             start=(kk == 0), stop=(kk == 7))
                                for kk in range(8):
                                    e.matmul(out=pBp,
                                             lhsT=wfb[par][:, kk * 512 + o2 * C: kk * 512 + o2 * C + C],
                                             rhs=xgb[:, kk * gtw + so_: kk * gtw + so_ + swd],
                                             start=(kk == 0), stop=(kk == 7))
                                e.drain().then_inc(st.h, 1)
                            T(_)
                            t_12 = st.inc(1)
                            h1_rd = (st, 0)

                            def _(e, o=o, swd=swd, t_12=t_12, pA=pA, tg=t1_rd):
                                e.wait_ge(st.h, t_12)
                                wt(e, tg)
                                wt(e, tB_tab)
                                e.activation(out=t1f[:, 0:swd], in_=pA,
                                             func=AF.Silu, scale=s1s[:, o:o + 1])
                                e.drain().then_inc(ss.h, 1)
                            SC(_)
                            t_sl = ss.inc(1)

                            def _(e, o=o, so_=so_, swd=swd, t_sl=t_sl, pBp=pBp, gtw=gtw):
                                e.wait_ge(ss.h, t_sl)
                                wt(e, tB_tab)
                                dsl = h1b[:, o * gtw + so_: o * gtw + so_ + swd]
                                e.tensor_scalar(out=dsl, in0=pBp,
                                                scalar1=s2s[:, o:o + 1], scalar2=None,
                                                op0=ALU.mult)
                                e.tensor_tensor(out=dsl, in0=dsl, in1=t1f[:, 0:swd], op=ALU.mult)
                                e.drain().then_inc(sv.h, 1)
                            V(_)
                            t_h1 = sv.inc(1)
                            t1_rd = (sv, t_h1)
                            pgu[(0, hf)] = (ss, t_sl)
                            pgu[(1, hf)] = (sv, t_h1)
                    wdbl_free["a"][par] = (st, st.n)
                    wdbl_free["b"][par] = (st, st.n)
                st1_mm_done = (st, st.n)
                st1_done = (sv, sv.n)
                xg_rd = st1_mm_done

                def hstage(srcb, dstb, wsrc, ssb, src_ready, dst_wr_guard, pidx):
                    for ob in range(32):
                        par = ob % 2
                        def _(e, g1=wdbl_free["h"][par]):
                            wt(e, g1)
                        G(_)
                        t_wh = dma("g", whb[par][:].rearrange("p (k d) -> p k d", k=32),
                                   wsrc[ob], s_wh[par])
                        for o2 in range(1):
                            o = ob
                            hf = ob % 2
                            for (so_, swd) in nsl:
                                pA = pB4[pidx][:, hf * 512: hf * 512 + swd]
                                def _(e, par=par, so_=so_, swd=swd, t_wh=t_wh,
                                      pA=pA, pbg=pgu[(pidx, hf)], sdr=src_ready, o2=o2):
                                    wt(e, t_wh)
                                    wt(e, sdr)
                                    wt(e, pbg)
                                    for kk in range(32):
                                        e.matmul(out=pA,
                                                 lhsT=whb[par][:, kk * C: kk * C + C],
                                                 rhs=srcb[:, kk * gtw + so_: kk * gtw + so_ + swd],
                                                 start=(kk == 0), stop=(kk == 31))
                                    e.drain().then_inc(st.h, 1)
                                T(_)
                                t_hm = st.inc(1)

                                def _(e, o=o, so_=so_, swd=swd, t_hm=t_hm, pA=pA,
                                      dwg=dst_wr_guard):
                                    e.wait_ge(st.h, t_hm)
                                    wt(e, dwg)
                                    wt(e, tB_tab)
                                    e.activation(out=dstb[:, o * gtw + so_: o * gtw + so_ + swd],
                                                 in_=pA, func=AF.Gelu, scale=ssb[:, o:o + 1])
                                    e.drain().then_inc(ss.h, 1)
                                SC(_)
                                pgu[(pidx, hf)] = (ss, ss.inc(1))
                                dst_wr_guard = (ss, 0)
                        wdbl_free["h"][par] = (st, st.n)
                    return (ss, ss.n), (st, st.n)

                st2_done, st2_mm = hstage(h1b, h2b, fh1, sh1s, st1_done, h2_rd, 2)
                st3_done, st3_mm = hstage(h2b, h1b, fh2, sh2s, st2_done, st2_mm, 3)
                h2_rd = st3_mm

                for tb0 in range(0, ntile, 3):
                    tbn = min(3, ntile - tb0)
                    mm_evs = []
                    for kb in range(8):
                        par = kb % 2
                        def _(e, g1=wdbl_free["w3"][par]):
                            wt(e, g1)
                        G(_)
                        t_w3 = dma("g", w3b[par][:].rearrange("p (k d) -> p k d", k=4),
                                   fw3[kb * 4:(kb + 1) * 4].rearrange("k p d -> p k d"),
                                   s_w3[par])
                        def _(e, kb=kb, par=par, tb0=tb0, tbn=tbn, t_w3=t_w3,
                              sd=st3_done, gds=[pgu[(i, hh)] for i in range(tbn)
                                                for hh in range(2)] if kb == 0 else []):
                            wt(e, t_w3)
                            wt(e, sd)
                            for g_ in gds:
                                wt(e, g_)
                            for ti in range(tbn):
                                tl = tb0 + ti
                                for hh in range(2):
                                    for kk in range(4):
                                        kg = kb * 4 + kk
                                        e.matmul(
                                            out=pB4[ti][:, hh * 512: hh * 512 + 512],
                                            lhsT=h1b[:, kg * gtw + tl * C: kg * gtw + tl * C + C],
                                            rhs=w3b[par][:, kk * D + hh * 512: kk * D + hh * 512 + 512],
                                            start=(kb == 0 and kk == 0),
                                            stop=(kb == 7 and kk == 3))
                            e.drain().then_inc(st.h, 1)
                        T(_)
                        mm_evs.append(st.inc(1))
                        wdbl_free["w3"][par] = (st, st.n)
                    t_s4 = mm_evs[-1]
                    h1_rd = (st, t_s4)

                    for ti in range(tbn):
                        tl_loc = tb0 + ti
                        tile = oc0 + tl_loc
                        par2 = tile % 2
                        def _(e, sf=strm2_free[par2]):
                            wt(e, sf)
                        SY(_)
                        t_xo = dma("y", xo2[par2][:], park1[tile], s_xo[par2])
                        g, t_psf = lnboundary2(tile, pB4[ti], t_s4, t_xo, xo2[par2],
                                               strm2[par2], sqd2, lnt2, xtt2,
                                               scale_tab=s3b, pgu=pgu)
                        strm2_free[par2] = g
                        pgu[(ti, 0)] = (sv, t_psf)
                        pgu[(ti, 1)] = (sv, t_psf)
            tB_parks = [(s_pk[0], s_pk[0].n), (s_pk[1], s_pk[1].n)]
            tB_strm = [strm2_free[0], strm2_free[1]]
            def _(e, a=st.n, b=ss.n, pk0=s_pk[0].n, pk1=s_pk[1].n):
                e.wait_ge(st.h, a)
                e.wait_ge(ss.h, b)
                e.wait_ge(s_pk[0].h, pk0)
                e.wait_ge(s_pk[1].h, pk1)
                e.drain().then_inc(sv.h, 1)
            V(_)
            tB_bar = (sv, sv.inc(1))
            if debug == 3:
                def _(e):
                    wt(e, tB_parks[0])
                    wt(e, tB_parks[1])
                    for t in range(OWNCH):
                        e.dma_start(out=y[t], in_=park1[t]).then_inc(s_y[0].h, 16)
                    e.wait_ge(s_y[0].h, s_y[0].inc(16 * OWNCH))
                SY(_)

        if debug == 3:
            return nc

        with contextlib.ExitStack() as esC:
            PADW = 64
            kfm = esC.enter_context(sbt("kfm", [C, 8 * (PADW + OWN)], BF16))
            v65 = esC.enter_context(sbt("v65", [C, OWNCH * H * 65], BF16))
            vpad = esC.enter_context(sbt("vpad", [C, H * 65], BF16))
            xl1 = esC.enter_context(sbt("xl1", [C, 8 * OWN], BF16))
            wl1 = [esC.enter_context(sbt(f"wl1_{i}", [C, 8 * D], BF16)) for i in range(2)]
            bl1 = esC.enter_context(sbt("bl1", [C, 8], F32))
            bl1k = esC.enter_context(sbt("bl1k", [C, 8], F32))
            bv1 = esC.enter_context(sbt("bv1", [1, D], BF16))
            bo1 = esC.enter_context(sbt("bo1", [1, D], BF16))
            qb1 = esC.enter_context(sbt("qb1", [1, D], BF16))
            qwsb = esC.enter_context(sbt("qwsb", [C, D], F32))
            wqz = esC.enter_context(sbt("wqz", [C, 8 * D], BF16))
            qfm = esC.enter_context(sbt("qfm", [C, 8 * C], BF16))
            pTs = esC.enter_context(sbt("pTs", [C, 1024], BF16))
            rec16 = esC.enter_context(sbt("rec16", [C, 16], F32))
            attnf = esC.enter_context(sbt("attnf", [C, D], F32))
            attnT = esC.enter_context(sbt("attnT", [C, 8 * C], BF16))
            xo3 = [esC.enter_context(sbt(f"xo3_{i}", [C, D], F32)) for i in range(2)]
            strm3 = [esC.enter_context(sbt(f"strm3_{i}", [C, D], F32)) for i in range(2)]
            sqd3 = esC.enter_context(sbt("sqd3", [C, D], BF16))
            lnt3 = esC.enter_context(sbt("lnt3", [C, 8], F32))
            xtt3 = esC.enter_context(sbt("xtt3", [C, 8 * C], BF16))
            xq32 = esC.enter_context(sbt("xq32", [C, D], I32))
            xqf = esC.enter_context(sbt("xqf", [C, D], F32))
            amx = esC.enter_context(sbt("amx", [C, 4], F32))
            pC4 = [esC.enter_context(pst("pC4_0", [C, 1024], F32)),
                   esC.enter_context(pst("pC4_1", [C, 1024], F32)),
                   esC.enter_context(pst("pC4_2", [C, 1536], F32))]
            qbb = esC.enter_context(sbt("qbb", [C, D], F32))
            amreg = esC.enter_context(sbt("amreg", [C, 64], F32))
            cm20 = esC.enter_context(sbt("cm20", [C, 1], F32))
            amb0 = esC.enter_context(sbt("amb0", [C, 64], F32))
            amb0t = esC.enter_context(sbt("amb0t", [C, 64], F32))

            def _(e):
                wt(e, tB_bar)
            SY(_)
            def _(e):
                wt(e, tB_bar)
            G(_)
            dma("y", bl1[:], abqc, s_pret)
            dma("y", bl1k[:], abkc, s_pret)
            dma("y", amreg[:], t_amreg, s_pret)
            dma("y", amb0[:], t_amb0, s_pret)
            dma("y", amb0t[:], t_amb0t, s_pret)
            dma("g", qwsb[:], qwsr.partition_broadcast(C), s_pre)
            dma("g", qbb[:], qbr.partition_broadcast(C), s_pre)
            t_bv1 = dma("g", bv1[:], abvr, s_pre)
            dma("g", bo1[:], abor, s_pre)
            t_wqz = dma("g", wqz[:].rearrange("p (k d) -> p k d", k=8),
                        qwq.rearrange("k p d -> p k d"), s_pre)
            tC_tab = (s_pret, s_pret.n)
            tC_tabw = (s_pre, s_pre.n)
            t_bv1 = tC_tabw
            t_wqz = tC_tabw

            def _(e):
                wt(e, tC_tab)
                wt(e, tC_tabw)
                e.memset(cm20[:], -20.0)
                e.memzero(vpad[:])
                e.tensor_scalar(out=bl1[:], in0=bl1[:], scalar1=0.125, scalar2=None,
                                op0=ALU.mult)
                e.drain().then_inc(sv.h, 1)
            V(_)
            tC_b = sv.inc(1)

            def _(e):
                wt(e, tB_parks[0])
                wt(e, tB_parks[1])
            SY(_)
            for oi in range(OWNCH):
                t_xl1 = dma("y",
                            xl1[:].rearrange("p (k t) -> p k t", k=8)[
                                :, :, oi * C:(oi + 1) * C],
                            xtsc[oi].rearrange("k p t -> p k t"), s_xl)

            t_wk1 = dma("g", wl1[0][:].rearrange("p (k d) -> p k d", k=8),
                        awk.rearrange("k p d -> p k d"), s_wl[0])
            KW = PADW + OWN
            pA_guard = [tB_bar, tB_bar, tB_bar, tB_bar]
            kv_evs = []
            for (so_, swd) in _nsl(OWN):
                for oc8 in range(8):
                    PK = pC4[oc8 % 2]
                    def _(e, so_=so_, swd=swd, oc8=oc8, PK=PK, t_wk1=t_wk1,
                          t_xl1=t_xl1, g_=pA_guard[oc8 % 2], kv=list(kv_evs)):
                        wt(e, t_wk1)
                        wt(e, t_xl1)
                        wt(e, g_)
                        if len(kv) >= 2:
                            wt(e, kv[-2])
                        for kk in range(8):
                            e.matmul(out=PK[:, 0:swd],
                                     lhsT=wl1[0][:, kk * D + oc8 * C: kk * D + oc8 * C + C],
                                     rhs=xl1[:, kk * OWN + so_: kk * OWN + so_ + swd],
                                     start=(kk == 0), stop=(kk == 7))
                        e.drain().then_inc(st.h, 1)
                    T(_)
                    t_kmm = st.inc(1)
                    pA_guard[oc8 % 2] = (st, t_kmm)

                    def _(e, so_=so_, swd=swd, oc8=oc8, PK=PK, t_kmm=t_kmm):
                        e.wait_ge(st.h, t_kmm)
                        wt(e, tC_tab)
                        e.activation(out=kfm[:, oc8 * KW + PADW + so_: oc8 * KW + PADW + so_ + swd],
                                     in_=PK[:, 0:swd], func=AF.Identity, bias=bl1k[:, oc8:oc8 + 1])
                        e.drain().then_inc(ss.h, 1)
                    SC(_)
                    kv_evs.append((ss, ss.inc(1)))
                    pA_guard[oc8 % 2] = kv_evs[-1]
            def _(e):
                for oc8 in range(8):
                    e.memzero(kfm[:, oc8 * KW: oc8 * KW + PADW])
                e.drain().then_inc(sv.h, 1)
            V(_)
            t_kpad = sv.inc(1)

            t_wv1 = dma("g", wl1[1][:].rearrange("p (k d) -> p k d", k=8),
                        awv.rearrange("k p d -> p k d"), s_wl[1])
            v_evs = []
            for tile in range(OWNCH):
                PV = pC4[tile % 2]
                def _(e, tile=tile, PV=PV, t_wv1=t_wv1, t_xl1=t_xl1, t_bv1=t_bv1,
                      g_=pA_guard[tile % 2]):
                    wt(e, t_wv1)
                    wt(e, t_bv1)
                    wt(e, t_xl1)
                    wt(e, g_)
                    for hh in range(2):
                        o = PV[:, hh * 512: hh * 512 + 512]
                        for kk in range(8):
                            e.matmul(out=o,
                                     lhsT=xl1[:, kk * OWN + tile * C: kk * OWN + tile * C + C],
                                     rhs=wl1[1][:, kk * D + hh * 512: kk * D + hh * 512 + 512],
                                     start=(kk == 0), stop=False)
                        e.matmul(out=o, lhsT=ones1[:], rhs=bv1[:, hh * 512: hh * 512 + 512],
                                 start=False, stop=True)
                    e.drain().then_inc(st.h, 1)
                T(_)
                t_vmm = st.inc(1)

                def _(e, tile=tile, PV=PV, t_vmm=t_vmm):
                    e.wait_ge(st.h, t_vmm)
                    dst = v65[:].rearrange("p (o h x) -> p o h x", o=OWNCH, h=H)[
                        :, tile, :, 0:64]
                    e.activation(out=dst,
                                 in_=PV[:].rearrange("p (h x) -> p h x", h=H),
                                 func=AF.Copy)
                    e.drain().then_inc(ss.h, 1)
                SC(_)
                t_vev = ss.inc(1)
                pA_guard[tile % 2] = (ss, t_vev)

                def _(e, tile=tile, t_vev=t_vev):
                    e.wait_ge(ss.h, t_vev)
                    dst1 = v65[:].rearrange("p (o h x) -> p o h x", o=OWNCH, h=H)[
                        :, tile, :, 64:65]
                    e.memset(dst1, 1.0)
                    e.drain().then_inc(sv.h, 1)
                V(_)
                v_evs.append((sv, sv.inc(1)))
            t_v_done = v_evs[-1]
            t_k_done = kv_evs[-1]

            def _(e, a=st.n):
                e.wait_ge(st.h, a)
            G(_)
            t_wq1 = dma("g", wl1[0][:].rearrange("p (k d) -> p k d", k=8),
                        awq.rearrange("k p d -> p k d"), s_wl[0])
            t_wo1 = dma("g", wl1[1][:].rearrange("p (k d) -> p k d", k=8),
                        awo.rearrange("k p d -> p k d"), s_wl[1])
            strm3_free = [(s_y[0], 0), (s_y[1], 0)]
            pB_guard = {0: pA_guard[0], 1: pA_guard[1], 2: tB_bar, 3: tB_bar}
            prev_exp = (ss, 0)
            prev_av = (st, 0)
            prev_att = (sv, 0)
            prev_int8 = (st, 0)
            prev_scores = (st, 0)
            prev_attT = (ss, 0)
            prev_q = (ss, 0)
            prev_xq = (ss, 0)
            for tile in range(OWNCH):
                PQ = pC4[0]
                def _(e, tile=tile, t_wq1=t_wq1, g_=pB_guard[0], pq=prev_q):
                    wt(e, t_wq1)
                    wt(e, g_)
                    wt(e, pq)
                    for oc8 in range(8):
                        for kk in range(8):
                            e.matmul(out=PQ[:, oc8 * C: oc8 * C + C],
                                     lhsT=wl1[0][:, kk * D + oc8 * C: kk * D + oc8 * C + C],
                                     rhs=xl1[:, kk * OWN + tile * C: kk * OWN + tile * C + C],
                                     start=(kk == 0), stop=(kk == 7))
                    e.drain().then_inc(st.h, 1)
                T(_)
                t_qm = st.inc(1)
                pB_guard[0] = (st, t_qm)

                def _(e, t_qm=t_qm, tC_b=tC_b, psc=prev_scores):
                    e.wait_ge(st.h, t_qm)
                    e.wait_ge(sv.h, tC_b)
                    wt(e, psc)
                    for oc8 in range(8):
                        e.activation(out=qfm[:, oc8 * C:(oc8 + 1) * C],
                                     in_=PQ[:, oc8 * C:(oc8 + 1) * C], func=AF.Identity,
                                     bias=bl1[:, oc8:oc8 + 1], scale=0.125)
                    e.drain().then_inc(ss.h, 1)
                SC(_)
                t_qe = ss.inc(1)
                prev_q = (ss, t_qe)

                for bl in range(2):
                    PS = pC4[1]
                    coloff = (bl % 2) * 0
                    kw0 = tile * C + bl * 64
                    def _(e, tile=tile, bl=bl, t_qe=t_qe, kw0=kw0,
                          g_=pB_guard[1], tk=t_k_done, tkp=t_kpad, pe=prev_exp):
                        e.wait_ge(ss.h, t_qe)
                        wt(e, tk)
                        e.wait_ge(sv.h, tkp)
                        wt(e, g_)
                        wt(e, pe)
                        for hpar in range(2):
                            r0 = hpar * 64
                            if hpar == 1:
                                e.drain()
                            for j in range(H // 2):
                                h_ = 2 * j + hpar
                                for half in range(2):
                                    ob = (64 - bl * 64 + half * 64) % 128
                                    e.matmul(out=PS[ob:ob + 64, h_ * 64:(h_ + 1) * 64],
                                             lhsT=kfm[r0:r0 + 64,
                                                      j * KW + kw0 + half * 64:
                                                      j * KW + kw0 + half * 64 + 64],
                                             rhs=qfm[r0:r0 + 64,
                                                     j * C + bl * 64: j * C + bl * 64 + 64],
                                             start=True, stop=True,
                                             tile_position=(r0, ob))
                        e.drain().then_inc(st.h, 1)
                    T(_)
                    t_sc = st.inc(1)
                    prev_scores = (st, t_sc)

                    if tile == 0:
                        def _(e, bl=bl, t_sc=t_sc):
                            e.wait_ge(st.h, t_sc)
                            wt(e, tC_tab)
                            am = (amb0t if bl == 0 else amreg)
                            e.tensor_tensor(
                                out=PS[:].rearrange("p (h x) -> p h x", h=H),
                                in0=PS[:].rearrange("p (h x) -> p h x", h=H),
                                in1=am[:].rearrange("p (a x) -> p a x", a=1).broadcast_to([C, H, 64]),
                                op=ALU.add)
                            e.drain().then_inc(sv.h, 1)
                        V(_)
                        t_am = sv.inc(1)
                    else:
                        def _(e, bl=bl, t_sc=t_sc):
                            e.wait_ge(st.h, t_sc)
                            wt(e, tC_tab)
                            am2 = (amb0 if bl == 0 else amreg)
                            e.tensor_tensor(
                                out=PS[:].rearrange("p (h x) -> p h x", h=H),
                                in0=PS[:].rearrange("p (h x) -> p h x", h=H),
                                in1=am2[:].rearrange("p (a x) -> p a x", a=1).broadcast_to([C, H, 64]),
                                op=ALU.add)
                            e.drain().then_inc(sv.h, 1)
                        V(_)
                        t_am = sv.inc(1)

                    def _(e, bl=bl, t_am=t_am, pav=prev_av):
                        e.wait_ge(sv.h, t_am)
                        wt(e, pav)
                        e.activation(out=pTs[:], in_=PS[:], func=AF.Exp, bias=cm20[:])
                        e.drain().then_inc(ss.h, 1)
                    SC(_)
                    t_exp = ss.inc(1)
                    prev_exp = (ss, t_exp)
                    pB_guard[1] = (ss, t_exp)

                    PAV = pC4[2]
                    def _(e, tile=tile, bl=bl, t_exp=t_exp, g_=pB_guard[2],
                          tv=t_v_done, pa=prev_att):
                        e.wait_ge(ss.h, t_exp)
                        wt(e, tv)
                        if bl == 0:
                            wt(e, g_)
                            wt(e, pa)
                        r0 = bl * 64
                        for h_ in range(H):
                            hs = slice(h_ * 65, (h_ + 1) * 65)
                            if bl == 0:
                                vt0 = (vpad if tile == 0 else
                                       v65[:, (tile - 1) * H * 65:(tile) * H * 65])
                                parts = ((64, vt0[64:128, hs]),
                                         (0, v65[0:64, tile * H * 65 + h_ * 65:
                                                 tile * H * 65 + (h_ + 1) * 65]))
                            else:
                                vt = v65[:, tile * H * 65:(tile + 1) * H * 65]
                                parts = ((0, vt[0:64, hs]), (64, vt[64:128, hs]))
                            if h_ % 2 == 1:
                                parts = parts[::-1]
                            avc = (h_ // 7) * 512 + (h_ % 7) * 66
                            for pi, (pb, vsrc) in enumerate(parts):
                                if pi == 1:
                                    e.drain()
                                e.matmul(out=PAV[r0:r0 + 64, avc:avc + 65],
                                         lhsT=pTs[pb:pb + 64, h_ * 64:(h_ + 1) * 64],
                                         rhs=vsrc, start=(pi == 0), stop=(pi == 1),
                                         tile_position=(pb, r0))
                        e.drain().then_inc(st.h, 1)
                    T(_)
                    t_av = st.inc(1)
                    pB_guard[2] = (st, t_av)
                    prev_av = (st, t_av)

                def _(e, t_av=pB_guard[2][1]):
                    e.wait_ge(st.h, t_av)
                    pv = pC4[2]
                    for (h0, nh) in ((0, 7), (7, 7), (14, 2)):
                        base = (h0 // 7) * 512
                        pg_ = pv[:, base:base + nh * 66].rearrange(
                            "p (h x) -> p h x", h=nh)[:, :, 0:65]
                        e.reciprocal(out=rec16[:, h0:h0 + nh].rearrange(
                            "p (h x) -> p h x", x=1), in_=pg_[:, :, 64:65])
                        e.tensor_tensor(
                            out=attnf[:, h0 * 64:(h0 + nh) * 64].rearrange(
                                "p (h x) -> p h x", h=nh),
                            in0=pg_[:, :, 0:64],
                            in1=rec16[:, h0:h0 + nh].rearrange(
                                "p (h x) -> p h x", x=1).broadcast_to([C, nh, 64]),
                            op=ALU.mult)
                    e.drain().then_inc(sv.h, 1)
                V(_)
                t_attn = sv.inc(1)
                prev_att = (sv, t_attn)

                def _(e, t_attn=t_attn, g_=pB_guard[1], pT_=prev_attT):
                    e.wait_ge(sv.h, t_attn)
                    wt(e, g_)
                    wt(e, pT_)
                    for kk in range(8):
                        e.transpose(out=pC4[1][:, kk * C:(kk + 1) * C],
                                    in_=attnf[:, kk * C:(kk + 1) * C], identity=ident[:])
                    e.drain().then_inc(st.h, 1)
                T(_)
                t_aT = st.inc(1)
                pB_guard[1] = (st, t_aT)

                def _(e, t_aT=t_aT, pi=prev_int8):
                    e.wait_ge(st.h, t_aT)
                    wt(e, pi)
                    e.activation(out=attnT[:], in_=pC4[1][:], func=AF.Copy)
                    e.drain().then_inc(ss.h, 1)
                SC(_)
                t_aTe = ss.inc(1)
                prev_attT = (ss, t_aTe)

                def _(e, t_aTe=t_aTe, t_wo1=t_wo1, g_=pB_guard[0]):
                    e.wait_ge(ss.h, t_aTe)
                    wt(e, t_wo1)
                    wt(e, g_)
                    for hh in range(2):
                        o = pC4[0][:, hh * 512: hh * 512 + 512]
                        for kk in range(8):
                            e.matmul(out=o, lhsT=attnT[:, kk * C:(kk + 1) * C],
                                     rhs=wl1[1][:, kk * D + hh * 512: kk * D + hh * 512 + 512],
                                     start=(kk == 0), stop=False)
                        e.matmul(out=o, lhsT=ones1[:], rhs=bo1[:, hh * 512: hh * 512 + 512],
                                 start=False, stop=True)
                    e.drain().then_inc(st.h, 1)
                T(_)
                t_om = st.inc(1)
                pB_guard[0] = (st, t_om)

                par3 = tile % 2
                def _(e, sf=strm3_free[par3]):
                    wt(e, sf)
                SY(_)
                t_xo3 = dma("y", xo3[par3][:], park1[tile], s_xo[par3])
                ST3 = strm3[par3]
                def _(e, t_om=t_om, t_xo3=t_xo3, ST3=ST3):
                    e.wait_ge(st.h, t_om)
                    wt(e, t_xo3)
                    e.tensor_tensor(out=ST3[:], in0=pC4[0][:], in1=xo3[par3][:], op=ALU.add)
                    e.tensor_reduce(out=lnt3[:, 0:1], in_=ST3[:], axis=AX.X, op=ALU.add)
                    e.drain().then_inc(sv.h, 1)
                V(_)
                t_c1 = sv.inc(1)
                pB_guard[0] = (sv, t_c1)

                def _(e, t_c1=t_c1, ST3=ST3):
                    e.wait_ge(sv.h, t_c1)
                    e.activation(out=sqd3[:], in_=ST3[:], func=AF.Square)
                    e.drain().then_inc(ss.h, 1)
                SC(_)
                t_c2 = ss.inc(1)

                def _(e, t_c2=t_c2, ST3=ST3):
                    e.wait_ge(ss.h, t_c2)
                    e.tensor_reduce(out=lnt3[:, 1:2], in_=sqd3[:], axis=AX.X, op=ALU.add)
                    e.tensor_scalar(out=lnt3[:, 0:2], in0=lnt3[:, 0:2], scalar1=1.0 / D,
                                    scalar2=None, op0=ALU.mult)
                    e.tensor_tensor(out=lnt3[:, 2:3], in0=lnt3[:, 0:1], in1=lnt3[:, 0:1], op=ALU.mult)
                    e.tensor_tensor(out=lnt3[:, 1:2], in0=lnt3[:, 1:2], in1=lnt3[:, 2:3], op=ALU.subtract)
                    e.tensor_scalar(out=lnt3[:, 1:2], in0=lnt3[:, 1:2], scalar1=1e-5,
                                    scalar2=None, op0=ALU.add)
                    e.drain().then_inc(sv.h, 1)
                V(_)
                t_c3 = sv.inc(1)

                def _(e, t_c3=t_c3):
                    e.wait_ge(sv.h, t_c3)
                    e.activation(out=lnt3[:, 1:2], in_=lnt3[:, 1:2], func=AF.Sqrt)
                    e.drain().then_inc(ss.h, 1)
                SC(_)
                t_c4 = ss.inc(1)

                def _(e, t_c4=t_c4, ST3=ST3):
                    e.wait_ge(ss.h, t_c4)
                    e.reciprocal(out=lnt3[:, 1:2], in_=lnt3[:, 1:2])
                    e.tensor_scalar(out=ST3[:], in0=ST3[:], scalar1=lnt3[:, 0:1],
                                    scalar2=lnt3[:, 1:2], op0=ALU.subtract, op1=ALU.mult)
                    e.tensor_reduce(out=amx[:, 0:1], in_=ST3[:], axis=AX.X, op=ALU.max,
                                    apply_absolute_value=True)
                    e.tensor_scalar(out=amx[:, 1:2], in0=amx[:, 0:1], scalar1=1e-6,
                                    scalar2=1.0 / 127.0, op0=ALU.max, op1=ALU.mult)
                    e.reciprocal(out=amx[:, 2:3], in_=amx[:, 1:2])
                    e.tensor_scalar(out=attnf[:], in0=ST3[:], scalar1=amx[:, 2:3],
                                    scalar2=None, op0=ALU.mult)
                    e.tensor_copy(out=xq32[:], in_=attnf[:])
                    e.tensor_copy(out=xqf[:], in_=xq32[:])
                    e.drain().then_inc(sv.h, 1)
                V(_)
                t_c5 = sv.inc(1)
                prev_xq = (sv, t_c5)
                dump_ev = (s_xl, 0)
                if debug == 4:
                    def _(e, t_c5=t_c5, tile=tile, ST3=ST3):
                        e.wait_ge(sv.h, t_c5)
                        e.dma_start(out=park1[tile], in_=ST3[:]).then_inc(s_xl.h, 16)
                    SY(_)
                    s_xl.inc(16)
                    dump_ev = (s_xl, s_xl.n)

                def _(e, t_c5=t_c5, g_=pB_guard[1]):
                    e.wait_ge(sv.h, t_c5)
                    wt(e, g_)
                    for kk in range(8):
                        e.transpose(out=pC4[1][:, kk * C:(kk + 1) * C],
                                    in_=xqf[:, kk * C:(kk + 1) * C], identity=ident[:])
                    e.drain().then_inc(st.h, 1)
                T(_)
                t_c6 = st.inc(1)

                def _(e, t_c6=t_c6):
                    e.wait_ge(st.h, t_c6)
                    e.activation(out=attnT[:], in_=pC4[1][:], func=AF.Copy)
                    e.drain().then_inc(ss.h, 1)
                SC(_)
                t_c7 = ss.inc(1)
                pB_guard[1] = (ss, t_c7)

                def _(e, t_c7=t_c7, t_wqz=t_wqz, g_=pB_guard[0]):
                    e.wait_ge(ss.h, t_c7)
                    wt(e, t_wqz)
                    wt(e, g_)
                    for hh in range(2):
                        o = pC4[0][:, hh * 512: hh * 512 + 512]
                        for kk in range(8):
                            e.matmul(out=o, lhsT=attnT[:, kk * C:(kk + 1) * C],
                                     rhs=wqz[:, kk * D + hh * 512: kk * D + hh * 512 + 512],
                                     start=(kk == 0), stop=(kk == 7))
                    e.drain().then_inc(st.h, 1)
                T(_)
                t_c8 = st.inc(1)
                pB_guard[0] = (st, t_c8)
                prev_int8 = (st, t_c8)

                def _(e, t_c8=t_c8, ST3=ST3, tC_tabw=tC_tabw):
                    e.wait_ge(st.h, t_c8)
                    wt(e, tC_tabw)
                    e.tensor_scalar(out=attnf[:], in0=pC4[0][:], scalar1=amx[:, 1:2],
                                    scalar2=None, op0=ALU.mult)
                    e.tensor_tensor(out=attnf[:], in0=attnf[:], in1=qwsb[:], op=ALU.mult)
                    e.drain().then_inc(sv.h, 1)
                V(_)
                t_c9 = sv.inc(1)
                pB_guard[0] = (sv, t_c9)

                def _(e, t_c9=t_c9, ST3=ST3, de=dump_ev):
                    e.wait_ge(sv.h, t_c9)
                    wt(e, de)
                    e.tensor_tensor(out=ST3[:], in0=attnf[:], in1=ST3[:], op=ALU.add)
                    e.tensor_tensor(out=ST3[:], in0=ST3[:], in1=qbb[:], op=ALU.add)
                    e.tensor_reduce(out=lnt3[:, 0:1], in_=ST3[:], axis=AX.X, op=ALU.add)
                    e.drain().then_inc(sv.h, 1)
                V(_)
                t_d1 = sv.inc(1)

                def _(e, t_d1=t_d1, ST3=ST3):
                    e.wait_ge(sv.h, t_d1)
                    e.activation(out=sqd3[:], in_=ST3[:], func=AF.Square)
                    e.drain().then_inc(ss.h, 1)
                SC(_)
                t_d2 = ss.inc(1)

                def _(e, t_d2=t_d2, ST3=ST3):
                    e.wait_ge(ss.h, t_d2)
                    e.tensor_reduce(out=lnt3[:, 1:2], in_=sqd3[:], axis=AX.X, op=ALU.add)
                    e.tensor_scalar(out=lnt3[:, 0:2], in0=lnt3[:, 0:2], scalar1=1.0 / D,
                                    scalar2=None, op0=ALU.mult)
                    e.tensor_tensor(out=lnt3[:, 2:3], in0=lnt3[:, 0:1], in1=lnt3[:, 0:1], op=ALU.mult)
                    e.tensor_tensor(out=lnt3[:, 1:2], in0=lnt3[:, 1:2], in1=lnt3[:, 2:3], op=ALU.subtract)
                    e.tensor_scalar(out=lnt3[:, 1:2], in0=lnt3[:, 1:2], scalar1=1e-5,
                                    scalar2=None, op0=ALU.add)
                    e.drain().then_inc(sv.h, 1)
                V(_)
                t_d3 = sv.inc(1)

                def _(e, t_d3=t_d3):
                    e.wait_ge(sv.h, t_d3)
                    e.activation(out=lnt3[:, 1:2], in_=lnt3[:, 1:2], func=AF.Sqrt)
                    e.drain().then_inc(ss.h, 1)
                SC(_)
                t_d4 = ss.inc(1)

                def _(e, t_d4=t_d4, ST3=ST3):
                    e.wait_ge(ss.h, t_d4)
                    e.reciprocal(out=lnt3[:, 1:2], in_=lnt3[:, 1:2])
                    e.tensor_scalar(out=ST3[:], in0=ST3[:], scalar1=lnt3[:, 0:1],
                                    scalar2=lnt3[:, 1:2], op0=ALU.subtract, op1=ALU.mult)
                    e.drain().then_inc(sv.h, 1)
                V(_)
                t_d5 = sv.inc(1)

                def _(e, t_d5=t_d5, tile=tile, ST3=ST3, sy3=s_y[par3]):
                    e.wait_ge(sv.h, t_d5)
                    e.dma_start(out=y[tile], in_=ST3[:]).then_inc(sy3.h, 16)
                SY(_)
                s_y[par3].inc(16)
                strm3_free[par3] = (s_y[par3], s_y[par3].n)

            def _(e, a=s_y[0].n, b=s_y[1].n):
                e.wait_ge(s_y[0].h, a)
                e.wait_ge(s_y[1].h, b)
            SY(_)

    return nc


def _prep_shared(inputs):
    f32 = lambda a: np.ascontiguousarray(np.asarray(a), dtype=np.float32)
    T = _host_tables()
    w = {}
    w["l0_wq"] = f32(inputs["L0_attn_Wq"]).reshape(8, C, D)
    w["l0_wk"] = f32(inputs["L0_attn_Wk"]).reshape(8, C, D)
    w["l0_wv"] = f32(inputs["L0_attn_Wv"]).reshape(8, C, D)
    w["l0_wg"] = f32(inputs["L0_attn_Wg"]).reshape(8, C, D)
    w["l0_wo"] = f32(inputs["L0_attn_Wo"]).reshape(8, C, D)
    w["l0_bq"] = f32(inputs["L0_attn_bq"]).reshape(1, D)
    w["l0_bk"] = f32(inputs["L0_attn_bk"]).reshape(1, D)
    w["l0_bv"] = f32(inputs["L0_attn_bv"]).reshape(1, D)
    w["l0_bgc"] = np.ascontiguousarray(f32(inputs["L0_attn_bg"]).reshape(8, C).T)
    w["l0_bo"] = f32(inputs["L0_attn_bo"]).reshape(1, D)
    gnw_hp = np.ascontiguousarray(f32(inputs["L0_attn_gn_w"]).reshape(8, 2).T)
    gnb_hp = np.ascontiguousarray(f32(inputs["L0_attn_gn_b"]).reshape(8, 2).T)
    w["l0_gnwb"] = np.ascontiguousarray(np.concatenate([gnw_hp, gnb_hp], axis=1))
    w1T = np.ascontiguousarray(f32(inputs["L0_ffn_w1_w"]).T)
    w2T = np.ascontiguousarray(f32(inputs["L0_ffn_w2_w"]).T)
    h1T = np.ascontiguousarray(f32(inputs["L0_ffn_h1_w"]).T)
    h2T = np.ascontiguousarray(f32(inputs["L0_ffn_h2_w"]).T)
    w3T = np.ascontiguousarray(f32(inputs["L0_ffn_w3_w"]).T)
    w["f_w1"] = np.ascontiguousarray(w1T.reshape(8, C, 8, 512).transpose(2, 0, 1, 3))
    w["f_w2"] = np.ascontiguousarray(w2T.reshape(8, C, 8, 512).transpose(2, 0, 1, 3))
    w["f_h1"] = np.ascontiguousarray(h1T.reshape(32, C, 32, C).transpose(2, 1, 0, 3))
    w["f_h2"] = np.ascontiguousarray(h2T.reshape(32, C, 32, C).transpose(2, 1, 0, 3))
    w["f_w3"] = w3T.reshape(32, C, D)
    w["f_s1"] = np.ascontiguousarray(f32(inputs["L0_ffn_w1_s"]).reshape(32, C).T)
    w["f_s2"] = np.ascontiguousarray(f32(inputs["L0_ffn_w2_s"]).reshape(32, C).T)
    w["f_sh1"] = np.ascontiguousarray(f32(inputs["L0_ffn_h1_s"]).reshape(32, C).T)
    w["f_sh2"] = np.ascontiguousarray(f32(inputs["L0_ffn_h2_s"]).reshape(32, C).T)
    w["f_s3"] = f32(inputs["L0_ffn_w3_s"]).reshape(1, D)
    w["l1_wq"] = np.ascontiguousarray(f32(inputs["L1_attn_Wq"]).T).reshape(8, C, D)
    w["l1_wk"] = np.ascontiguousarray(f32(inputs["L1_attn_Wk"]).T).reshape(8, C, D)
    w["l1_wv"] = np.ascontiguousarray(f32(inputs["L1_attn_Wv"]).T).reshape(8, C, D)
    w["l1_wo"] = np.ascontiguousarray(f32(inputs["L1_attn_Wo"]).T).reshape(8, C, D)
    w["l1_bqc"] = np.ascontiguousarray(f32(inputs["L1_attn_bq"]).reshape(8, C).T)
    w["l1_bkc"] = np.ascontiguousarray(f32(inputs["L1_attn_bk"]).reshape(8, C).T)
    w["l1_bv"] = f32(inputs["L1_attn_bv"]).reshape(1, D)
    w["l1_bo"] = f32(inputs["L1_attn_bo"]).reshape(1, D)
    w["q_w"] = np.ascontiguousarray(np.asarray(inputs["L1_ffn_weight_q"]).T).reshape(8, C, D)
    w["q_ws"] = f32(inputs["L1_ffn_weight_scale"]).reshape(1, D)
    w["q_b"] = f32(inputs["L1_ffn_bias"]).reshape(1, D)
    w["t_qcos"] = T["rot_t"]["qcos"]
    w["t_qsin"] = T["rot_t"]["qsin"]
    w["t_kcos"] = T["rot_t"]["kcos"]
    w["t_ksin"] = T["rot_t"]["ksin"]
    w["t_qtab"] = T["qtab"]
    w["t_k1"] = T["k1tab"]
    w["t_ktf"] = T["ktabf"]
    w["t_sdec"] = T["sdec"]
    w["t_cm"] = T["cm"]
    w["t_ind2"] = T["ind2"]
    w["t_indT2"] = T["indT2"]
    w["t_ones1"] = T["ones1"]
    w["t_ident"] = T["ident"]
    w["t_amreg"] = T["am_reg"]
    w["t_amb0"] = T["am_b0"]
    w["t_amb0t"] = T["am_b0t"]
    return w, T


def _percore(inputs, w, T, core):
    b, r = core // 4, core % 4
    xb = np.ascontiguousarray(np.asarray(inputs["L0_attn_x"][b], dtype=np.float32))
    xT = np.ascontiguousarray(xb.reshape(NCH, C, 8, C).transpose(0, 2, 3, 1))
    os_ = OWN_START[r]
    cs = os_ // C
    m = dict(w)
    m["xT_seq"] = xT
    m["x_own"] = np.ascontiguousarray(xb[os_:os_ + OWN].reshape(OWNCH, C, D))
    m["xT_own"] = np.ascontiguousarray(xT[cs:cs + OWNCH])
    for nm, tb, wdt in (("o_qcos", "t_qcos", ROTD), ("o_qsin", "t_qsin", ROTD),
                        ("o_kcos", "t_kcos", ROTD), ("o_ksin", "t_ksin", ROTD),
                        ("o_qtab", "t_qtab", H)):
        m[nm] = np.ascontiguousarray(w[tb][:, cs * wdt:(cs + OWNCH) * wdt])
    coefv = np.zeros((C, 3), np.float32)
    if r > 0:
        coefv[:, r - 1] = 1.0
    m["t_coef"] = coefv
    return m


_NC_CACHE = {}


def kernel(**inputs):
    w, T = _prep_shared(inputs)
    key = DBG
    if key not in _NC_CACHE:
        _NC_CACHE[key] = build(debug=DBG)
    nc = _NC_CACHE[key]
    in_maps = [_percore(inputs, w, T, c) for c in range(8)]
    res = run_bass_kernel_spmd(nc, in_maps, core_ids=list(range(8)))
    out = np.empty((B, S, D), np.float32)
    for c in range(8):
        b, r = c // 4, c % 4
        yv = np.asarray(res.results[c]["y"]).reshape(OWN, D)
        if r == 0:
            out[b, 0:2048] = yv[0:2048]
        else:
            out[b, 2048 * r:2048 * (r + 1)] = yv[C:OWN]
    if DBG:
        kernel.dbg = [{k: np.asarray(v) for k, v in res.results[c].items()}
                      for c in range(8)]
    return out
